# version 07
# baseline (speedup 1.0000x reference)
import contextlib

import numpy as np

import concourse.bass as bass
import concourse.mybir as mybir
from concourse.bass_utils import run_bass_kernel_spmd

F32 = mybir.dt.float32
BF16 = mybir.dt.bfloat16
I32 = mybir.dt.int32
ALU = mybir.AluOpType
ACT = mybir.ActivationFunctionType
AXL = mybir.AxisListType

NCORES = 8
S = 8192
D = 2048
TOK = 1024

NEG30 = -1.0e30
PEN = -1.0e27
RSQ = float(1.0 / np.sqrt(128.0))

TAPS = ()
STAGES = "all"


class K:

    def __init__(self, nc):
        self.nc = nc
        self.ctx = contextlib.ExitStack()
        keys = ["t", "v", "s", "g"]
        self.inc_amt = {"t": 1, "v": 1, "s": 1, "g": 1}
        for b in ("q", "gd"):
            for ln in range(4):
                keys.append(f"{b}{ln}")
                self.inc_amt[f"{b}{ln}"] = 16
        self.rot = {"q": 0, "gd": 0}
        self.sem = {k: self.ctx.enter_context(nc.semaphore("S" + k)) for k in keys}
        self.eng = {}
        self.exp = {k: 0 for k in keys}
        self.seq = {k: 0 for k in keys}
        self.last = {k: None for k in keys}
        self.incs = {k: [] for k in keys}
        self.wm = {}
        self.writer = {}
        self.readers = {}
        self.dirty = {k: set() for k in keys}

    def _cover(self, ek, seq):
        for val, sq in self.incs[ek]:
            if sq >= seq:
                return val
        self.exp[ek] += self.inc_amt[ek]
        self.last[ek].then_inc(self.sem[ek], self.inc_amt[ek])
        self.incs[ek].append((self.exp[ek], self.seq[ek]))
        return self.exp[ek]

    def _wait(self, cons, ek, val):
        key = (cons, ek)
        if self.wm.get(key, -1) >= val:
            return
        self.eng[cons].wait_ge(self.sem[ek], val)
        self.wm[key] = val

    def dep(self, key, names_in, names_out):
        need_drain = False
        for nm in names_in:
            w = self.writer.get(nm)
            if w is None:
                continue
            wek, wseq = w
            if wek == key:
                if nm in self.dirty[key] and key in ("v", "s", "g"):
                    need_drain = True
            else:
                self._wait(key, wek, self._cover(wek, wseq))
        for nm in names_out:
            for rek, rseq in list(self.readers.get(nm, {}).items()):
                if rek != key:
                    self._wait(key, rek, self._cover(rek, rseq))
            w = self.writer.get(nm)
            if w is not None and w[0] != key:
                self._wait(key, w[0], self._cover(w[0], w[1]))
        if need_drain:
            self.eng[key].drain()
            self.dirty[key].clear()

    def did(self, key, inst, names_in, names_out):
        self.seq[key] += 1
        self.last[key] = inst
        for nm in names_in:
            self.readers.setdefault(nm, {})[key] = self.seq[key]
        for nm in names_out:
            self.writer[nm] = (key, self.seq[key])
            self.readers[nm] = {}
            self.dirty[key].add(nm)

    def op(self, key, fn, ins, outs):
        ni = [a.tensor.name for a in ins]
        no = [a.tensor.name for a in outs]
        self.dep(key, ni, no)
        inst = fn(self.eng[key])
        self.did(key, inst, ni, no)
        return inst

    def dma(self, base, out, in_):
        ln = self.rot[base]
        self.rot[base] = (ln + 1) % 4
        key = f"{base}{ln}"
        inst = self.op(key, lambda e: e.dma_start(out=out, in_=in_), [in_], [out])
        self._cover(key, self.seq[key])
        return inst

    def mm(self, out, lhsT, rhs, start=True, stop=True):
        return self.op("t", lambda e: e.matmul(out, lhsT, rhs, start=start, stop=stop),
                       [lhsT, rhs], [out])

    def tp(self, out, in_, ident):
        p = in_.shape[0]
        idap = ident if p == ident.shape[0] else ident[0:p, 0:p]
        return self.op("t", lambda e: e.transpose(out, in_, idap), [in_, idap], [out])

    def cp(self, key, out, in_):
        return self.op(key, lambda e: e.tensor_copy(out, in_), [in_], [out])

    def tt(self, key, out, a, b, alu):
        return self.op(key, lambda e: e.tensor_tensor(out, a, b, alu), [a, b], [out])

    def ts(self, key, out, in_, s1, s2, op0, op1=ALU.bypass, accum=None):
        ins = [in_] + [x for x in (s1, s2) if isinstance(x, bass.AP)]
        outs = [out] + ([accum] if accum is not None else [])
        return self.op(key, lambda e: e.tensor_scalar(out, in_, s1, s2, op0, op1,
                                                      accum_out=accum), ins, outs)

    def stt(self, key, out, in0, scalar, in1, op0, op1):
        ins = [in0, in1] + ([scalar] if isinstance(scalar, bass.AP) else [])
        return self.op(key, lambda e: e.scalar_tensor_tensor(out, in0, scalar, in1,
                                                             op0, op1), ins, [out])

    def act(self, out, in_, func, bias=0.0, scale=1.0, accum=None):
        ins = [in_] + [x for x in (bias, scale) if isinstance(x, bass.AP)]
        outs = [out] + ([accum] if accum is not None else [])
        return self.op("s", lambda e: e.activation(out, in_, func, bias=bias,
                                                   scale=scale, accum_out=accum),
                       ins, outs)

    def recip(self, out, in_):
        return self.op("v", lambda e: e.reciprocal(out, in_), [in_], [out])

    def red(self, key, out, in_, alu, absolute=False, neg=False):
        return self.op(key, lambda e: e.tensor_reduce(out, in_, AXL.X, alu,
                                                      apply_absolute_value=absolute,
                                                      negate=neg), [in_], [out])

    def memset(self, key, ap, val):
        return self.op(key, lambda e: e.memset(ap, val), [], [ap])

    def iota(self, ap, pattern, base=0, mult=0):
        return self.op("g", lambda e: e.iota(ap, pattern=pattern, base=base,
                                             channel_multiplier=mult,
                                             allow_small_or_imprecise_dtypes=True),
                       [], [ap])

    def asel(self, out, in_, pattern, fill, base, mult):
        return self.op("g", lambda e: e.affine_select(out=out, in_=in_, pattern=pattern,
                                                      compare_op=ALU.is_ge, fill=fill,
                                                      base=base, channel_multiplier=mult),
                       [in_], [out])

    def coll(self, kind, ins, outs):
        def f(e):
            return e.collective_compute(kind, ALU.bypass,
                                        replica_groups=[list(range(NCORES))],
                                        ins=ins, outs=outs)
        inst = self.op("g", f, ins, outs)
        self._cover("g", self.seq["g"])
        return inst


def build_program(taps):
    nc = bass.Bass("TRN2", target_bir_lowering=False)
    k = K(nc)

    def din(name, shape, dtype=F32):
        return nc.dram_tensor(name, shape, dtype, kind="ExternalInput").ap()

    def dout(name, shape, dtype=F32):
        return nc.dram_tensor(name, shape, dtype, kind="ExternalOutput").ap()

    def dtmp(name, shape, dtype=F32):
        return nc.dram_tensor(name, shape, dtype, kind="Internal").ap()

    def sb(name, shape, dtype=F32):
        return nc.alloc_sbuf_tensor(name, shape, dtype).ap()

    def ps(name):
        return nc.alloc_psum_tensor(name, [128, 512], F32).ap()

    x0 = din("x", [TOK, D])
    thr = din("thr", [TOK, 1])
    mcs = din("mcs", [1, 8])
    mbd = din("mbd", [1, 8])
    wdq = din("wdq", [D, 512]); wuq = din("wuq", [512, 1024])
    wdkv = din("wdkv", [D, 512]); wuk = din("wuk", [512, 1024])
    wuv = din("wuv", [512, D]); wqr = din("wqr", [512, 1024])
    wkr = din("wkr", [D, 1024]); wout0 = din("wout0", [D, D])
    gq = din("gq", [128, 4]); gkv = din("gkv", [128, 4]); bout0 = din("bout0", [128, 16])
    gwT = din("gwT", [D, 8])
    efc1T = din("efc1T", [8, D, 1408]); efc2T = din("efc2T", [8, D, 1408])
    eprojT = din("eprojT", [8, 1408, D])
    sfc1T = din("sfc1T", [D, 5632]); sfc2T = din("sfc2T", [D, 5632])
    sprojT = din("sprojT", [5632, D]); sgwT = din("sgwT", [D, 1])
    wqkvz = din("wqkvz", [D, 12288])
    wbab = din("wbab", [D, 32]); wbaa = din("wbaa", [D, 32])
    cwt_in = din("cwt", [128, 256])
    alog = din("alog", [32, 1]); dtb = din("dtb", [32, 1]); nw_in = din("nw", [1, 128])
    wout1 = din("wout1", [4096, D])
    wgT = din("wgT", [D, 8192]); wvT = din("wvT", [D, 8192]); woT = din("woT", [8192, D])
    bg = din("bg", [128, 64]); bv = din("bv", [128, 64]); bo = din("bo", [128, 16])

    out_o = dout("o", [TOK, D])
    tap_t = {}
    tshapes = {"x1": [TOK, D], "x2": [TOK, D], "x3": [TOK, D],
               "cq": [128, 4096], "qkvc": [64, 128, 1024], "s0": [128, 4096],
               "csrep": [128, 1024], "comb": [8, 1024], "ao": [16, 128, 1024],
               "dl0": [128, 8192], "dl1": [128, 8192]}
    for name in taps:
        tap_t[name] = dout("tap_" + name, tshapes[name])

    cs_dram = dtmp("cs_dram", [32, 1024])
    gt_dram = dtmp("gt_dram", [32, 1024])
    qt_dram = dtmp("qt_dram", [16, 128, 1024])
    kag_in = dtmp("kag_in", [16, 128, 1024])
    kag_out = dtmp("kag_out", [8, 16, 128, 1024])
    vag_in = dtmp("vag_in", [1024, D])
    vag_out = dtmp("vag_out", [8, 1024, D])
    attn_o = dtmp("attn_o", [16, 128, 1024])
    x1_dram = dtmp("x1_dram", [TOK, D])
    x2_dram = dtmp("x2_dram", [TOK, D])
    x3_dram = dtmp("x3_dram", [TOK, D])
    comb_dram = dtmp("comb_dram", [8, 1024])
    mixed_dram = dtmp("mixed_dram", [64, 128, 1024])
    zT_dram = dtmp("zT_dram", [32, 128, 1024])
    bnd_in = dtmp("bnd_in", [128, 192])
    bnd_out = dtmp("bnd_out", [8, 128, 192])
    qkvc_dram = dtmp("qkvc_dram", [64, 128, 1024])
    kn_dram = dtmp("kn_dram", [8, 32, 128, 128])
    xi_dram = dtmp("xi_dram", [8, 32, 128, 128])
    rl_dram = dtmp("rl_dram", [8, 32, 128, 128])
    mc_dram = dtmp("mc_dram", [8, 32, 128, 128])
    sl_dram = dtmp("sl_dram", [8, 32, 128, 128])
    sp_in = dtmp("sp_in", [2, 32, 128, 128])
    sp_out = dtmp("sp_out", [8, 2, 32, 128, 128])
    z_dram = dtmp("z_dram", [64, 128, 512])
    wqg_dram = dtmp("wqg_dram", [D, 8192], BF16)
    wqv_dram = dtmp("wqv_dram", [D, 8192], BF16)
    wqo_dram = dtmp("wqo_dram", [8192, D], BF16)

    A = sb("A", [128, 16384])
    B = sb("B", [128, 12288])
    C = sb("C", [128, 8192])
    ident = sb("ident", [128, 128])
    identb = sb("identb", [128, 128], BF16)
    lowneg = sb("lowneg", [128, 128])
    upneg = sb("upneg", [128, 128])
    trilinc = sb("trilinc", [128, 128])
    kidx = sb("kidx", [128, 512])
    csrep = sb("csrep", [128, 1024])
    nwbc = sb("nwbc", [128, 128])
    ones_r = sb("ones_r", [1, 128])
    ones_c = sb("ones_c", [128, 1])
    c127 = sb("c127", [128, 1])
    ceps5 = sb("ceps5", [128, 1])
    ceps6 = sb("ceps6", [128, 1])
    gq_sb = sb("gq_sb", [128, 4]); gkv_sb = sb("gkv_sb", [128, 4])
    bout0_sb = sb("bout0_sb", [128, 16])
    bg_sb = sb("bg_sb", [128, 64]); bv_sb = sb("bv_sb", [128, 64])
    bo_sb = sb("bo_sb", [128, 16])
    bcm_cs = sb("bcm_cs", [128, 8])
    bcm_bd = sb("bcm_bd", [128, 8])
    thr_sb = sb("thr_sb", [128, 8])
    wsta = sb("wsta", [128, 512]); wstb = sb("wstb", [128, 512])
    wqbf = sb("wqbf", [128, 512], BF16)
    isc = sb("isc", [128, 512], I32)
    qsc = sb("qsc", [128, 512])
    zqbf = sb("zqbf", [128, 512], BF16)
    BFq = sb("BFq", [128, 8192], BF16)
    gT = sb("gT", [32, 1024])
    bT = sb("bT", [32, 1024])
    GT = sb("GT", [32, 1024])
    G_tm = sb("G_tm", [128, 256])
    b_tm = sb("b_tm", [128, 256])
    cwt = sb("cwt_sb", [128, 256])
    rws = sb("rws", [1, 1024])
    shg_sb = sb("shg_sb", [1, 1024])
    crow = None
    nw_sb = sb("nw_sb", [1, 128])
    ost = sb("ost", [128, 512])
    pts = ost[:, 256:384]
    gw_g = sb("gw_g", [128, 1]); gw_v = sb("gw_v", [128, 1]); gw_o = sb("gw_o", [128, 1])
    alog_sb = sb("alog_sb", [32, 1])
    dtb_sb = sb("dtb_sb", [32, 1])
    thp = sb("thp", [32, 1]); thc = sb("thc", [32, 1]); ths = sb("ths", [32, 1])
    hpi = sb("hpi", [32, 1])
    W0 = None
    col = {nm: sb("col_" + nm, [128, 1]) for nm in
           ["negmax", "rsum", "gx", "sx", "gz", "sxz", "t0", "t1",
            "gcb", "eg", "dec", "egc", "ssq", "rstd", "nexa", "m1", "m2"]}

    P = [ps(f"P{i}") for i in range(8)]
    crow = rws
    W0 = [C[:, i * 128:(i + 1) * 128] for i in range(22)]
    bndb = C[:, 2112:2304]
    bndl = C[:, 2304:2496]

    for ln in range(4):
        k.eng[f"q{ln}"] = nc.sync
        k.eng[f"gd{ln}"] = nc.gpsimd
    k.eng["g"] = nc.gpsimd
    k.eng["v"] = nc.vector
    k.eng["s"] = nc.scalar
    k.eng["t"] = nc.tensor

    def bcast_row(row_ap, n, dst_sb, pbank):
        for j0 in range(0, n, 512):
            w = min(512, n - j0)
            k.mm(pbank[0:128, j0 % 512:j0 % 512 + w], ones_r, row_ap[0:1, j0:j0 + w])
            k.cp("v", dst_sb[:, j0:j0 + w], pbank[0:128, j0 % 512:j0 % 512 + w])

    def bcast_col(src11, dst_col, pbank):
        k.mm(pbank[0:128, 0:1], ones_r, src11)
        k.cp("v", dst_col, pbank[0:128, 0:1])

    def tp128(dst_sb, src_sb, pb, bf=False):
        k.tp(pb, src_sb, identb if bf else ident)
        k.cp("v", dst_sb, pb)

    def xt_slice(kk, th):
        return A[:, th * 8192 + kk * 512: th * 8192 + kk * 512 + 512]

    k.memset("v", ones_r, 1.0)
    k.memset("v", ones_c, 1.0)
    k.memset("v", c127, 127.0)
    k.memset("v", ceps5, 1e-5)
    k.memset("v", ceps6, 1e-6)
    k.memset("g", ident, 1.0)
    k.asel(ident, ident, [[1, 128]], 0.0, 0, -1)
    k.asel(ident, ident, [[-1, 128]], 0.0, 0, 1)
    k.cp("v", identb, ident)
    k.memset("g", lowneg, 0.0)
    k.asel(lowneg, lowneg, [[1, 128]], NEG30, -1, -1)
    k.memset("g", upneg, 0.0)
    k.asel(upneg, upneg, [[-1, 128]], NEG30, 0, 1)
    k.memset("g", trilinc, 1.0)
    k.asel(trilinc, trilinc, [[1, 128]], 0.0, 0, -1)
    k.iota(kidx, [[1, 512]], 0, 0)
    k.dma("q", cwt, cwt_in)
    k.dma("q", nw_sb[0:1, 0:8], mcs)
    k.dma("q", nw_sb[0:1, 8:16], mbd)
    k.dma("q", alog_sb, alog)
    k.dma("q", dtb_sb, dtb)
    for qt in range(8):
        k.dma("q", thr_sb[:, qt:qt + 1], thr[qt * 128:qt * 128 + 128, :])
    for si in range(8):
        k.mm(P[3][0:128, si:si + 1], ones_r, nw_sb[0:1, si:si + 1])
        k.mm(P[4][0:128, si:si + 1], ones_r, nw_sb[0:1, 8 + si:9 + si])
    k.cp("v", bcm_cs, P[3][0:128, 0:8])
    k.cp("v", bcm_bd, P[4][0:128, 0:8])
    k.dma("q", nw_sb, nw_in)
    bcast_row(nw_sb, 128, nwbc, P[3])
    k.dma("q", gq_sb, gq)
    k.dma("q", gkv_sb, gkv)
    k.dma("q", bout0_sb, bout0)
    k.dma("q", bg_sb, bg)
    k.dma("q", bv_sb, bv)
    k.dma("q", bo_sb, bo)

    csC = A[0:32, 0:8192]
    csS = A[0:32, 8192:16384]
    tmpa = B[0:32, 0:4096]
    tmpb = B[0:32, 4096:8192]
    cs_own = B[0:32, 8192:9216]
    k.iota(thp, [[0, 1]], 0, 1)
    k.act(thp, thp, ACT.Exp, scale=float(-np.log(10000.0) / 64.0))
    k.memset("v", hpi, float(np.pi / 2))
    k.act(thc, thp, ACT.Sin, bias=hpi)
    k.act(ths, thp, ACT.Sin)
    k.memset("v", csC[:, 0:1], 1.0)
    k.memset("v", csS[:, 0:1], 0.0)
    k.cp("v", csC[:, 1:2], thc)
    k.cp("v", csS[:, 1:2], ths)
    n = 1
    while n < S - 1:
        m = min(n, S - 1 - n)
        cn = csC[:, n:n + 1]
        sn = csS[:, n:n + 1]
        k.ts("v", tmpa[:, 0:m], csC[:, 1:1 + m], cn, None, ALU.mult)
        k.ts("v", tmpb[:, 0:m], csS[:, 1:1 + m], sn, None, ALU.mult)
        k.tt("v", csC[:, n + 1:n + 1 + m], tmpa[:, 0:m], tmpb[:, 0:m], ALU.subtract)
        k.ts("v", tmpa[:, 0:m], csS[:, 1:1 + m], cn, None, ALU.mult)
        k.ts("v", tmpb[:, 0:m], csC[:, 1:1 + m], sn, None, ALU.mult)
        k.tt("v", csS[:, n + 1:n + 1 + m], tmpa[:, 0:m], tmpb[:, 0:m], ALU.add)
        n *= 2
    k.tt("v", csC, csC, csS, ALU.add)
    k.memset("v", cs_own, 0.0)
    for si in range(8):
        k.stt("v", cs_own, csC[:, 1024 * si:1024 * si + 1024],
              bcm_cs[0:32, si:si + 1], cs_own, ALU.mult, ALU.add)
    k.dma("gd", cs_dram, cs_own)
    for r in range(4):
        k.dma("q", csrep[32 * r:32 * r + 32, :], cs_dram)
    if "csrep" in tap_t:
        k.dma("gd", tap_t["csrep"], csrep)

    def proj(wap, kdim, mdim, rhs_fn, evac, mgroup=4):
        for m0 in range(0, mdim, mgroup):
            g = min(mgroup, mdim - m0)
            for kk in range(kdim):
                wst = wsta if kk % 2 == 0 else wstb
                k.dma("q", wst[:, 0:g * 128],
                      wap[kk * 128:kk * 128 + 128, m0 * 128:m0 * 128 + g * 128])
                for mi in range(g):
                    for th in range(2):
                        k.mm(P[mi * 2 + th], wst[:, mi * 128:mi * 128 + 128],
                             rhs_fn(kk, th), start=(kk == 0), stop=(kk == kdim - 1))
            for mi in range(g):
                for th in range(2):
                    evac(m0 + mi, th, P[mi * 2 + th])

    def ln_step(th, x_src, x_dst, build_xt):
        for jj in range(4):
            j = th * 4 + jj
            xld = B[:, 0:2048]
            k.dma("q", xld, x_src[j * 128:j * 128 + 128, :])
            for m in range(16):
                sl = (m % 4) * 128
                k.tp(P[2][:, sl:sl + 128],
                     C[:, m * 512 + jj * 128:m * 512 + jj * 128 + 128], ident)
                k.cp("v", wsta[:, sl:sl + 128], P[2][:, sl:sl + 128])
                if m % 4 == 3:
                    mm0 = (m // 4) * 512
                    k.tt("v", xld[:, mm0:mm0 + 512], xld[:, mm0:mm0 + 512],
                         wsta, ALU.add)
            k.red("v", col["t0"], xld, ALU.add, neg=True)
            k.ts("v", col["t1"], col["t0"], float(1.0 / 2048.0), None, ALU.mult)
            k.ts("v", xld, xld, col["t1"], None, ALU.add)
            k.act(B[:, 2048:4096], xld, ACT.Square, accum=col["ssq"])
            k.act(col["rstd"], col["ssq"], ACT.Sqrt, bias=ceps5, scale=float(1.0 / 2048.0))
            k.recip(col["rstd"], col["rstd"])
            k.ts("v", xld, xld, col["rstd"], None, ALU.mult)
            k.dma("gd", x_dst[j * 128:j * 128 + 128, :], xld)
            if build_xt:
                for kk in range(16):
                    sl = (kk % 4) * 128
                    k.tp(P[2][:, sl:sl + 128], xld[:, kk * 128:kk * 128 + 128], ident)
                    k.cp("v", A[:, th * 8192 + kk * 512 + jj * 128:
                                th * 8192 + kk * 512 + jj * 128 + 128],
                         P[2][:, sl:sl + 128])

    def l0_attn():
        for j in range(8):
            th, jj = j // 4, j % 4
            xld = B[:, 0:2048]
            k.dma("q", xld, x0[j * 128:j * 128 + 128, :])
            for kk in range(16):
                sl = (kk % 4) * 128
                k.tp(P[2][:, sl:sl + 128], xld[:, kk * 128:kk * 128 + 128], ident)
                k.cp("v", A[:, th * 8192 + kk * 512 + jj * 128:
                            th * 8192 + kk * 512 + jj * 128 + 128],
                     P[2][:, sl:sl + 128])

        def c_ev(base):
            def ev(m, th, pb):
                k.cp("v", B[:, base + th * 2048 + m * 512:
                            base + th * 2048 + m * 512 + 512], pb)
            return ev

        def rms_apply(base, gvec):
            for th in range(2):
                pb = P[4] if th == 0 else P[5]
                for m in range(4):
                    t = B[:, base + th * 2048 + m * 512: base + th * 2048 + m * 512 + 512]
                    k.act(wsta, t, ACT.Square)
                    k.mm(pb[0:1, 0:512], ones_c, wsta,
                         start=(m == 0), stop=(m == 3))
                k.act(rws[0:1, th * 512:th * 512 + 512], pb[0:1, 0:512], ACT.Sqrt,
                      bias=ceps5[0:1, :], scale=float(1.0 / 512.0))
                k.recip(rws[0:1, th * 512:th * 512 + 512],
                        rws[0:1, th * 512:th * 512 + 512])
            bcast_row(rws, 1024, C[:, 1024:2048], P[5])
            for th in range(2):
                for m in range(4):
                    t = B[:, base + th * 2048 + m * 512: base + th * 2048 + m * 512 + 512]
                    k.tt("v", t, t, C[:, 1024 + th * 512:1024 + th * 512 + 512],
                         ALU.mult)
                    k.ts("v", t, t, gvec[:, m:m + 1], None, ALU.mult)

        def cq_slice(kk, th):
            return B[:, th * 2048 + kk * 512: th * 2048 + kk * 512 + 512]

        def ckv_slice(kk, th):
            return B[:, 4096 + th * 2048 + kk * 512: 4096 + th * 2048 + kk * 512 + 512]

        proj(wdq, 16, 4, xt_slice, c_ev(0))
        rms_apply(0, gq_sb)
        proj(wdkv, 16, 4, xt_slice, c_ev(4096))
        rms_apply(4096, gkv_sb)
        if "cq" in tap_t:
            for th in range(2):
                for m in range(4):
                    k.dma("gd", tap_t["cq"][:, th * 2048 + m * 512:
                                            th * 2048 + m * 512 + 512],
                          cq_slice(m, th))

        def qb_ev(m, th, pb):
            k.ts("v", ost, pb, RSQ, None, ALU.mult)
            k.dma("gd", qt_dram[2 * m, 0:64, th * 512:th * 512 + 512], ost[0:64, 0:512])
            k.dma("gd", qt_dram[2 * m + 1, 0:64, th * 512:th * 512 + 512],
                  ost[64:128, 0:512])

        def qr_ev(m, th, pb):
            k.stt("v", ost, pb, RSQ, csrep[:, th * 512:th * 512 + 512],
                  ALU.mult, ALU.mult)
            k.dma("gd", qt_dram[2 * m, 64:128, th * 512:th * 512 + 512], ost[0:64, 0:512])
            k.dma("gd", qt_dram[2 * m + 1, 64:128, th * 512:th * 512 + 512],
                  ost[64:128, 0:512])

        def kb_ev(m, th, pb):
            k.cp("v", ost, pb)
            k.dma("gd", kag_in[2 * m, 0:64, th * 512:th * 512 + 512], ost[0:64, 0:512])
            k.dma("gd", kag_in[2 * m + 1, 0:64, th * 512:th * 512 + 512],
                  ost[64:128, 0:512])

        def kr_ev(m, th, pb):
            k.tt("v", ost, pb, csrep[:, th * 512:th * 512 + 512], ALU.mult)
            k.dma("gd", kag_in[2 * m, 64:128, th * 512:th * 512 + 512], ost[0:64, 0:512])
            k.dma("gd", kag_in[2 * m + 1, 64:128, th * 512:th * 512 + 512],
                  ost[64:128, 0:512])

        def v_ev(m, th, pb):
            k.cp("v", wsta, pb)
            for j in range(4):
                k.tp(P[6][:, j * 128:j * 128 + 128], wsta[:, j * 128:j * 128 + 128],
                     ident)
            k.cp("v", wstb, P[6])
            for j in range(4):
                k.dma("gd", vag_in[th * 512 + j * 128: th * 512 + j * 128 + 128,
                                   m * 128:m * 128 + 128], wstb[:, j * 128:j * 128 + 128])

        proj(wuq, 4, 8, cq_slice, qb_ev)
        proj(wqr, 4, 8, cq_slice, qr_ev)
        proj(wuk, 4, 8, ckv_slice, kb_ev)
        proj(wkr, 16, 8, xt_slice, kr_ev)
        proj(wuv, 4, 16, ckv_slice, v_ev, mgroup=3)

        k.coll("AllGather", [kag_in], [kag_out])
        k.coll("AllGather", [vag_in], [vag_out])

        SS = A[:, 8192:16384]
        for h in range(16):
            for si in range(8):
                k.dma("q", B[:, si * 1024:si * 1024 + 1024], kag_out[si, h])
            for kb in range(64):
                si, t0 = kb // 8, (kb % 8) * 128
                k.dma("q", A[:, kb * 128:kb * 128 + 128],
                      vag_out[si, t0:t0 + 128, h * 128:h * 128 + 128])
            k.dma("q", C[:, 0:1024], qt_dram[h])
            for qt in range(8):
                for kc in range(16):
                    pb = P[0] if kc % 2 == 0 else P[1]
                    k.mm(pb, C[:, qt * 128:qt * 128 + 128],
                         B[:, kc * 512:kc * 512 + 512])
                    k.ts("v", wsta, kidx, thr_sb[:, qt:qt + 1], float(512.0 * kc),
                         ALU.subtract, ALU.add)
                    k.ts("v", wsta, wsta, 0.0, None, ALU.max)
                    k.stt("v", SS[:, kc * 512:kc * 512 + 512], wsta, PEN, pb,
                          ALU.mult, ALU.add)
                k.red("v", col["negmax"], SS, ALU.max, neg=True)
                k.act(SS, SS, ACT.Exp, bias=col["negmax"], accum=col["rsum"])
                k.recip(col["rsum"], col["rsum"])
                k.tp(P[4][0:1, 0:128], col["rsum"], ident)
                k.cp("v", rws[0:1, 0:128], P[4][0:1, 0:128])
                k.mm(P[3][0:128, 0:128], ones_r, rws[0:1, 0:128])
                k.cp("v", ost[:, 128:256], P[3][0:128, 0:128])
                for kb in range(64):
                    k.tp(P[2][:, 0:128], SS[:, kb * 128:kb * 128 + 128], ident)
                    k.cp("v", pts, P[2][:, 0:128])
                    k.mm(P[5][:, 0:128], A[:, kb * 128:kb * 128 + 128], pts,
                         start=(kb == 0), stop=(kb == 63))
                k.tt("v", ost[:, 0:128], P[5][:, 0:128], ost[:, 128:256], ALU.mult)
                k.dma("gd", attn_o[h][:, qt * 128:qt * 128 + 128], ost[:, 0:128])

        if "ao" in tap_t:
            for h in range(16):
                k.dma("gd", tap_t["ao"][h], attn_o[h])
        for h in range(16):
            for th in range(2):
                k.dma("q", A[:, th * 8192 + h * 512:th * 8192 + h * 512 + 512],
                      attn_o[h][:, th * 512:th * 512 + 512])
        for th in range(2):
            for m0 in range(0, 16, 4):
                for kk in range(16):
                    wst = wsta if kk % 2 == 0 else wstb
                    k.dma("q", wst, wout0[kk * 128:kk * 128 + 128,
                                          m0 * 128:m0 * 128 + 512])
                    for mi in range(4):
                        k.mm(P[mi], wst[:, mi * 128:mi * 128 + 128],
                             A[:, th * 8192 + kk * 512:th * 8192 + kk * 512 + 512],
                             start=(kk == 0), stop=(kk == 15))
                for mi in range(4):
                    m = m0 + mi
                    k.ts("v", C[:, m * 512:m * 512 + 512], P[mi],
                         bout0_sb[:, m:m + 1], None, ALU.add)
            if f"dl{th}" in tap_t:
                k.dma("gd", tap_t[f"dl{th}"], C)
            ln_step(th, x0, x1_dram, build_xt=True)
        if "x1" in tap_t:
            k.dma("gd", tap_t["x1"], x1_dram)

    def l0_ffn():
        rtr = B[:, 6144:7168]
        for th in range(2):
            pb = P[4] if th == 0 else P[5]
            for kk in range(16):
                wst = wsta if kk % 2 == 0 else wstb
                k.dma("q", wst[:, 0:8], gwT[kk * 128:kk * 128 + 128, :])
                k.mm(pb[0:8, 0:512], wst[:, 0:8],
                     xt_slice(kk, th), start=(kk == 0), stop=(kk == 15))
            k.cp("v", rtr[0:8, th * 512:th * 512 + 512], pb[0:8, 0:512])
        ctm = B[:, 7168:7176]
        cT = B[:, 7680:8704]
        for j in range(8):
            k.tp(P[2][0:128, 0:8], rtr[0:8, j * 128:j * 128 + 128], ident)
            k.cp("v", ctm, P[2][0:128, 0:8])
            k.red("v", col["t0"], ctm, ALU.max, neg=True)
            k.act(ctm, ctm, ACT.Exp, bias=col["t0"], accum=col["rsum"])
            k.recip(col["rsum"], col["rsum"])
            k.ts("v", ctm, ctm, col["rsum"], None, ALU.mult)
            k.red("v", col["m1"], ctm, ALU.max)
            k.ts("v", wsta[:, 0:8], ctm, col["m1"], None, ALU.is_ge)
            k.stt("v", wsta[:, 8:16], wsta[:, 0:8], NEG30, ctm, ALU.mult, ALU.add)
            k.red("v", col["m2"], wsta[:, 8:16], ALU.max)
            k.ts("v", wsta[:, 0:8], ctm, col["m2"], None, ALU.is_ge)
            k.tt("v", ctm, ctm, wsta[:, 0:8], ALU.mult)
            k.tp(P[2][0:8, 0:128], ctm, ident)
            k.cp("v", cT[0:8, j * 128:j * 128 + 128], P[2][0:8, 0:128])
        k.dma("gd", comb_dram, cT[0:8, :])
        if "comb" in tap_t:
            k.dma("gd", tap_t["comb"], comb_dram)

        for th in range(2):
            pb = P[4] if th == 0 else P[5]
            for kk in range(16):
                wst = wsta if kk % 2 == 0 else wstb
                k.dma("q", wst[:, 0:1], sgwT[kk * 128:kk * 128 + 128, :])
                k.mm(pb[0:1, 0:512], wst[:, 0:1],
                     xt_slice(kk, th), start=(kk == 0), stop=(kk == 15))
            k.act(shg_sb[0:1, th * 512:th * 512 + 512], pb[0:1, 0:512], ACT.Sigmoid)

        for th in range(2):
            first = [True]

            def ffn_branch(wf1, wf2, wpj, nf, gate_row):
                bcast_row(gate_row[0:1, th * 512:th * 512 + 512], 512,
                          B[:, 5632:6144], P[3])
                for g0 in range(0, nf, 11):
                    gl = min(11, nf - g0)
                    for ft in range(gl):
                        f = g0 + ft
                        for kk in range(16):
                            wst = wsta if kk % 2 == 0 else wstb
                            k.dma("q", wst[:, 0:128],
                                  wf1[kk * 128:kk * 128 + 128, f * 128:f * 128 + 128])
                            k.dma("q", wst[:, 128:256],
                                  wf2[kk * 128:kk * 128 + 128, f * 128:f * 128 + 128])
                            k.mm(P[0], wst[:, 0:128], xt_slice(kk, th),
                                 start=(kk == 0), stop=(kk == 15))
                            k.mm(P[1], wst[:, 128:256], xt_slice(kk, th),
                                 start=(kk == 0), stop=(kk == 15))
                        Ht = B[:, ft * 512:ft * 512 + 512]
                        k.act(Ht, P[0], ACT.Silu)
                        k.tt("v", Ht, Ht, P[1], ALU.mult)
                        k.tt("v", Ht, Ht, B[:, 5632:6144], ALU.mult)
                    for m0 in range(0, 16, 4):
                        for ft in range(gl):
                            f = g0 + ft
                            wst = wsta if ft % 2 == 0 else wstb
                            k.dma("q", wst, wpj[f * 128:f * 128 + 128,
                                                m0 * 128:m0 * 128 + 512])
                            for mi in range(4):
                                k.mm(P[mi], wst[:, mi * 128:mi * 128 + 128],
                                     B[:, ft * 512:ft * 512 + 512],
                                     start=(ft == 0), stop=(ft == gl - 1))
                        for mi in range(4):
                            yt = C[:, (m0 + mi) * 512:(m0 + mi) * 512 + 512]
                            if first[0]:
                                k.cp("v", yt, P[mi])
                            else:
                                k.tt("v", yt, yt, P[mi], ALU.add)
                    first[0] = False

            for e in range(8):
                k.dma("q", crow, comb_dram[e:e + 1, :])
                ffn_branch(efc1T[e], efc2T[e], eprojT[e], 11, crow)
            ffn_branch(sfc1T, sfc2T, sprojT, 44, shg_sb)
            ln_step(th, x1_dram, x2_dram, build_xt=True)
        if "x2" in tap_t:
            k.dma("gd", tap_t["x2"], x2_dram)

    def l1_attn():
        for nm, wap, dst in (("b", wbab, bT), ("a", wbaa, gT)):
            for th in range(2):
                pb = P[4] if th == 0 else P[5]
                for kk in range(16):
                    wst = wsta if kk % 2 == 0 else wstb
                    k.dma("q", wst[:, 0:32], wap[kk * 128:kk * 128 + 128, :])
                    k.mm(pb[0:32, 0:512], wst[:, 0:32],
                         xt_slice(kk, th), start=(kk == 0), stop=(kk == 15))
                if nm == "b":
                    k.act(dst[0:32, th * 512:th * 512 + 512], pb[0:32, 0:512],
                          ACT.Sigmoid)
                else:
                    sl = dst[0:32, th * 512:th * 512 + 512]
                    k.act(sl, pb[0:32, 0:512], ACT.Exp, bias=dtb_sb)
                    k.act(sl, sl, ACT.Ln, bias=ones_c[0:32, :])
        k.act(col["nexa"][0:32, :], alog_sb, ACT.Exp)
        k.ts("v", col["nexa"][0:32, :], col["nexa"][0:32, :], -1.0, None, ALU.mult)
        k.ts("v", gT, gT, col["nexa"][0:32, :], None, ALU.mult)
        for c in range(8):
            k.tp(P[2][0:128, 0:32], gT[0:32, c * 128:c * 128 + 128], ident)
            k.cp("v", wsta[:, 0:32], P[2][0:128, 0:32])
            k.mm(P[4][0:128, 0:32], trilinc, wsta[:, 0:32])
            k.cp("v", G_tm[:, c * 32:c * 32 + 32], P[4][0:128, 0:32])
            k.tp(P[2][0:32, 0:128], G_tm[:, c * 32:c * 32 + 32], ident)
            k.cp("v", GT[0:32, c * 128:c * 128 + 128], P[2][0:32, 0:128])
            k.tp(P[2][0:128, 0:32], bT[0:32, c * 128:c * 128 + 128], ident)
            k.cp("v", b_tm[:, c * 32:c * 32 + 32], P[2][0:128, 0:32])
        k.dma("gd", gt_dram, GT)

        def qkv_ev(m, th, pb):
            k.cp("v", wstb, pb)
            k.dma("gd", mixed_dram[m][:, th * 512:th * 512 + 512], wstb)
            if th == 1:
                k.dma("gd", bnd_in[:, m * 3:m * 3 + 3], wstb[:, 509:512])

        def z_ev(m, th, pb):
            k.act(wstb, pb, ACT.Silu)
            k.dma("gd", zT_dram[m][:, th * 512:th * 512 + 512], wstb)

        proj(wqkvz[:, 0:8192], 16, 64, xt_slice, qkv_ev)
        proj(wqkvz[:, 8192:12288], 16, 32, xt_slice, z_ev)

        k.coll("AllGather", [bnd_in], [bnd_out])
        k.memset("v", bndb, 0.0)
        for si in range(8):
            k.dma("q", bndl, bnd_out[si])
            k.stt("v", bndb, bndl, bcm_bd[:, si:si + 1], bndb, ALU.mult, ALU.add)

        cbuf = C[:, 0:1027]
        cvo = C[:, 1040:2064]
        for ft in range(64):
            k.dma("q", cbuf[:, 3:1027], mixed_dram[ft])
            k.cp("v", cbuf[:, 0:3], bndb[:, ft * 3:ft * 3 + 3])
            k.ts("v", cvo, cbuf[:, 0:1024], cwt[:, ft * 4:ft * 4 + 1], None, ALU.mult)
            for j in range(1, 4):
                k.stt("v", cvo, cbuf[:, j:j + 1024], cwt[:, ft * 4 + j:ft * 4 + j + 1],
                      cvo, ALU.mult, ALU.add)
            k.act(cvo, cvo, ACT.Silu)
            k.dma("gd", qkvc_dram[ft], cvo)
        if "qkvc" in tap_t:
            for ft in range(64):
                k.dma("gd", tap_t["qkvc"][ft], qkvc_dram[ft])

        (kTr, k_tm, v_tm, kT_n, d_sb, e_sb, uoff, ut_sb, x_sb, xt_sb, z_sb,
         rl_sb, ek_sb, w2_sb, kd_sb, sl_sb, t2_sb, mc_sb, dmp, q_tm, qT_n,
         rh_sb) = W0
        spanM = B[:, 0:4096]
        spanS = B[:, 4096:8192]

        for c in range(8):
            for h in range(32):
                gcol = G_tm[:, c * 32 + h:c * 32 + h + 1]
                grow = rws[0:1, 0:128]
                k.dma("q", grow, gt_dram[h:h + 1, c * 128:c * 128 + 128])
                bcol = b_tm[:, c * 32 + h:c * 32 + h + 1]
                k.dma("q", kTr, qkvc_dram[16 + h // 2][:, c * 128:c * 128 + 128])
                k.dma("q", d_sb, qkvc_dram[32 + h][:, c * 128:c * 128 + 128])
                tp128(k_tm, kTr, P[2][:, 0:128])
                k.act(dmp, k_tm, ACT.Square, accum=col["ssq"])
                k.act(col["rstd"], col["ssq"], ACT.Sqrt, bias=ceps6)
                k.recip(col["rstd"], col["rstd"])
                k.ts("v", k_tm, k_tm, col["rstd"], None, ALU.mult)
                tp128(kT_n, k_tm, P[2][:, 0:128])
                k.dma("gd", kn_dram[c, h], kT_n)
                tp128(v_tm, d_sb, P[2][:, 0:128])
                k.mm(P[5][:, 0:128], kT_n, kT_n)
                k.mm(P[4][:, 0:128], ones_r, grow)
                k.cp("v", col["gcb"], P[4][:, 127:128])
                k.ts("v", d_sb, P[4][:, 0:128], gcol, None, ALU.subtract)
                k.tt("v", d_sb, d_sb, lowneg, ALU.add)
                k.act(e_sb, d_sb, ACT.Exp)
                k.tt("v", uoff, e_sb, P[5][:, 0:128], ALU.mult)
                k.ts("v", uoff, uoff, bcol, None, ALU.mult)
                k.tt("v", x_sb, ident, uoff, ALU.subtract)
                k.tp(P[2][:, 0:128], uoff, ident)
                k.cp("v", ut_sb, P[2][:, 0:128])
                k.tt("v", ut_sb, ut_sb, ident, ALU.add)
                for _ in range(6):
                    tp128(xt_sb, x_sb, P[2][:, 0:128])
                    k.mm(P[6][:, 0:128], ut_sb, x_sb)
                    k.stt("v", z_sb, ident, 2.0, P[6][:, 0:128], ALU.mult, ALU.subtract)
                    k.mm(P[6][:, 0:128], xt_sb, z_sb)
                    k.cp("v", x_sb, P[6][:, 0:128])
                k.dma("gd", xi_dram[c, h], x_sb)
                k.act(col["eg"], gcol, ACT.Exp)
                k.act(col["dec"], gcol, ACT.Exp, bias=col["gcb"], scale=-1.0)
                k.act(col["egc"], col["gcb"], ACT.Exp)
                k.mm(P[5][:, 0:128], x_sb, v_tm)
                k.ts("v", rl_sb, P[5][:, 0:128], bcol, None, ALU.mult)
                k.dma("gd", rl_dram[c, h], rl_sb)
                k.ts("v", ek_sb, k_tm, col["eg"], None, ALU.mult)
                k.mm(P[6][:, 0:128], x_sb, ek_sb)
                k.ts("v", w2_sb, P[6][:, 0:128], bcol, None, ALU.mult)
                k.ts("v", kd_sb, k_tm, col["dec"], None, ALU.mult)
                k.mm(P[7][:, 0:128], kd_sb, rl_sb)
                k.cp("v", sl_sb, P[7][:, 0:128])
                k.dma("gd", sl_dram[c, h], sl_sb)
                k.mm(P[5][:, 0:128], kd_sb, w2_sb)
                k.ts("v", t2_sb, ident, col["egc"], None, ALU.mult)
                k.tt("v", mc_sb, t2_sb, P[5][:, 0:128], ALU.subtract)
                k.dma("gd", mc_dram[c, h], mc_sb)
                sM = spanM[:, h * 128:h * 128 + 128]
                sS = spanS[:, h * 128:h * 128 + 128]
                if c == 0:
                    k.cp("v", sM, mc_sb)
                    k.cp("v", sS, sl_sb)
                else:
                    tp128(xt_sb, mc_sb, P[2][:, 0:128])
                    k.mm(P[6][:, 0:128], xt_sb, sS)
                    k.tt("v", sS, P[6][:, 0:128], sl_sb, ALU.add)
                    k.mm(P[7][:, 0:128], xt_sb, sM)
                    k.cp("v", sM, P[7][:, 0:128])

        for h in range(32):
            k.dma("gd", sp_in[0, h], spanM[:, h * 128:h * 128 + 128])
            k.dma("gd", sp_in[1, h], spanS[:, h * 128:h * 128 + 128])
        k.coll("AllGather", [sp_in], [sp_out])
        S0reg = B[:, 4096:8192]
        spf = B[:, 0:4096]
        for h in range(32):
            s0 = S0reg[:, h * 128:h * 128 + 128]
            pf = spf[:, h * 128:h * 128 + 128]
            k.memset("v", s0, 0.0)
            k.memset("v", pf, 0.0)
            for si in range(8):
                k.stt("v", s0, pf, bcm_cs[:, si:si + 1], s0, ALU.mult, ALU.add)
                if si < 7:
                    k.dma("q", kTr, sp_out[si, 0, h])
                    k.dma("q", k_tm, sp_out[si, 1, h])
                    if si == 0:
                        k.cp("v", pf, k_tm)
                    else:
                        tp128(xt_sb, kTr, P[2][:, 0:128])
                        k.mm(P[6][:, 0:128], xt_sb, pf)
                        k.tt("v", pf, P[6][:, 0:128], k_tm, ALU.add)
        if "s0" in tap_t:
            k.dma("gd", tap_t["s0"], S0reg)

        coreT = C[:, 4096:8192]
        for c in range(8):
            for h in range(32):
                gcol = G_tm[:, c * 32 + h:c * 32 + h + 1]
                grow = rws[0:1, 0:128]
                k.dma("q", grow, gt_dram[h:h + 1, c * 128:c * 128 + 128])
                bcol = b_tm[:, c * 32 + h:c * 32 + h + 1]
                s0 = S0reg[:, h * 128:h * 128 + 128]
                k.dma("q", kTr, qkvc_dram[h // 2][:, c * 128:c * 128 + 128])
                k.dma("q", kT_n, kn_dram[c, h])
                k.dma("q", x_sb, xi_dram[c, h])
                k.dma("q", rl_sb, rl_dram[c, h])
                k.act(col["eg"], gcol, ACT.Exp)
                tp128(q_tm, kTr, P[2][:, 0:128])
                k.act(dmp, q_tm, ACT.Square, accum=col["ssq"])
                k.act(col["rstd"], col["ssq"], ACT.Sqrt, bias=ceps6)
                k.recip(col["rstd"], col["rstd"])
                k.ts("v", q_tm, q_tm, col["rstd"], RSQ, ALU.mult, ALU.mult)
                tp128(qT_n, q_tm, P[2][:, 0:128])
                k.mm(P[5][:, 0:128], kT_n, s0)
                k.ts("v", ek_sb, P[5][:, 0:128], col["eg"], None, ALU.mult)
                k.mm(P[6][:, 0:128], x_sb, ek_sb)
                k.ts("v", w2_sb, P[6][:, 0:128], bcol, None, ALU.mult)
                k.tt("v", rh_sb, rl_sb, w2_sb, ALU.subtract)
                k.mm(P[5][:, 0:128], qT_n, s0)
                k.ts("v", z_sb, P[5][:, 0:128], col["eg"], None, ALU.mult)
                k.mm(P[6][:, 0:128], qT_n, kT_n)
                k.mm(P[4][:, 0:128], ones_r, grow)
                k.ts("v", d_sb, P[4][:, 0:128], gcol, None, ALU.subtract)
                k.stt("v", d_sb, d_sb, -1.0, upneg, ALU.mult, ALU.add)
                k.act(e_sb, d_sb, ACT.Exp)
                k.tt("v", e_sb, e_sb, P[6][:, 0:128], ALU.mult)
                tp128(xt_sb, e_sb, P[2][:, 0:128])
                k.mm(P[7][:, 0:128], xt_sb, rh_sb)
                k.tt("v", z_sb, z_sb, P[7][:, 0:128], ALU.add)
                k.act(dmp, z_sb, ACT.Square, accum=col["ssq"])
                k.act(col["rstd"], col["ssq"], ACT.Sqrt, bias=ceps6, scale=float(1.0 / 128.0))
                k.recip(col["rstd"], col["rstd"])
                k.ts("v", z_sb, z_sb, col["rstd"], None, ALU.mult)
                k.tt("v", z_sb, z_sb, nwbc, ALU.mult)
                k.dma("q", e_sb, zT_dram[h][:, c * 128:c * 128 + 128])
                tp128(d_sb, e_sb, P[2][:, 0:128])
                k.tt("v", z_sb, z_sb, d_sb, ALU.mult)
                k.tp(P[2][:, 0:128], z_sb, ident)
                k.cp("v", coreT[:, h * 128:h * 128 + 128], P[2][:, 0:128])
                k.dma("q", mc_sb, mc_dram[c, h])
                k.dma("q", sl_sb, sl_dram[c, h])
                tp128(xt_sb, mc_sb, P[2][:, 0:128])
                k.mm(P[6][:, 0:128], xt_sb, s0)
                k.tt("v", s0, P[6][:, 0:128], sl_sb, ALU.add)
            for m0 in (0, 8):
                for vh in range(32):
                    k.dma("q", wsta, wout1[vh * 128:vh * 128 + 128,
                                           m0 * 128:m0 * 128 + 512])
                    k.dma("q", wstb, wout1[vh * 128:vh * 128 + 128,
                                           m0 * 128 + 512:m0 * 128 + 1024])
                    for mi in range(4):
                        k.mm(P[mi][:, 0:128], wsta[:, mi * 128:mi * 128 + 128],
                             coreT[:, vh * 128:vh * 128 + 128],
                             start=(vh == 0), stop=(vh == 31))
                        k.mm(P[mi + 4][:, 0:128], wstb[:, mi * 128:mi * 128 + 128],
                             coreT[:, vh * 128:vh * 128 + 128],
                             start=(vh == 0), stop=(vh == 31))
                for mi in range(8):
                    k.cp("v", C[:, (m0 + mi) * 128:(m0 + mi) * 128 + 128],
                         P[mi][:, 0:128])
            ln_chunk(c)
        if "x3" in tap_t:
            k.dma("gd", tap_t["x3"], x3_dram)

    def ln_chunk(c):
        xld = C[:, 2048:4096]
        k.dma("q", xld, x2_dram[c * 128:c * 128 + 128, :])
        for m in range(16):
            sl = (m % 4) * 128
            k.tp(P[2][:, sl:sl + 128], C[:, m * 128:m * 128 + 128], ident)
            k.cp("v", wsta[:, sl:sl + 128], P[2][:, sl:sl + 128])
            if m % 4 == 3:
                mm0 = (m // 4) * 512
                k.tt("v", xld[:, mm0:mm0 + 512], xld[:, mm0:mm0 + 512], wsta, ALU.add)
        k.red("v", col["t0"], xld, ALU.add, neg=True)
        k.ts("v", col["t1"], col["t0"], float(1.0 / 2048.0), None, ALU.mult)
        k.ts("v", xld, xld, col["t1"], None, ALU.add)
        k.act(C[:, 0:2048], xld, ACT.Square, accum=col["ssq"])
        k.act(col["rstd"], col["ssq"], ACT.Sqrt, bias=ceps5, scale=float(1.0 / 2048.0))
        k.recip(col["rstd"], col["rstd"])
        k.ts("v", xld, xld, col["rstd"], None, ALU.mult)
        k.dma("gd", x3_dram[c * 128:c * 128 + 128, :], xld)

    def l1_ffn():
        def gamma_of(wap, kk_tiles, width, gw_col):
            for i in range(kk_tiles):
                k.dma("q", A[:, 0:width], wap[i * 128:i * 128 + 128, :])
                k.red("v", col["t0"], A[:, 0:width], ALU.add, absolute=True)
                if i == 0:
                    k.cp("v", col["t1"], col["t0"])
                else:
                    k.tt("v", col["t1"], col["t1"], col["t0"], ALU.add)
            k.mm(P[4][0:1, 0:1], ones_c, col["t1"])
            k.ts("v", rws[0:1, 0:1], P[4][0:1, 0:1], float(1.0 / (D * 8192)), 1e-5,
                 ALU.mult, ALU.max)
            bcast_col(rws[0:1, 0:1], gw_col, P[3])

        gamma_of(wgT, 16, 8192, gw_g)
        gamma_of(wvT, 16, 8192, gw_v)
        gamma_of(woT, 64, 2048, gw_o)

        def quant_w(wap, dst, kk_tiles, width, gw_col):
            k.recip(col["t0"], gw_col)
            for i in range(kk_tiles):
                for j0 in range(0, width, 512):
                    k.dma("q", wsta, wap[i * 128:i * 128 + 128, j0:j0 + 512])
                    k.ts("v", qsc, wsta, col["t0"], None, ALU.mult)
                    k.cp("v", isc, qsc)
                    k.cp("v", qsc, isc)
                    k.ts("v", qsc, qsc, 1.0, -1.0, ALU.min, ALU.max)
                    k.cp("v", wqbf, qsc)
                    k.dma("gd", dst[i * 128:i * 128 + 128, j0:j0 + 512], wqbf)

        quant_w(wgT, wqg_dram, 16, 8192, gw_g)
        quant_w(wvT, wqv_dram, 16, 8192, gw_v)
        quant_w(woT, wqo_dram, 64, 2048, gw_o)

        for th in range(2):
            for jj in range(4):
                j = th * 4 + jj
                xld = B[:, 0:2048]
                k.dma("q", xld, x3_dram[j * 128:j * 128 + 128, :])
                k.red("v", col["gx"], xld, ALU.max, absolute=True)
                k.ts("v", col["gx"], col["gx"], 1e-5, None, ALU.max)
                k.recip(col["sx"], col["gx"])
                k.ts("v", col["sx"], col["sx"], 127.0, None, ALU.mult)
                k.ts("v", col["t0"], col["gx"], float(1.0 / 127.0), None, ALU.mult)
                k.tp(P[4][0:1, jj * 128:jj * 128 + 128], col["t0"], ident)
                for p in range(4):
                    k.ts("v", qsc, xld[:, p * 512:p * 512 + 512], col["sx"], None,
                         ALU.mult)
                    k.cp("v", isc, qsc)
                    k.cp("v", qsc, isc)
                    for b2 in range(4):
                        kk = p * 4 + b2
                        k.tp(P[2][:, 0:128], qsc[:, b2 * 128:b2 * 128 + 128], ident)
                        k.cp("v", BFq[:, kk * 512 + jj * 128:kk * 512 + jj * 128 + 128],
                             P[2][:, 0:128])
            k.cp("v", rws[0:1, 0:512], P[4][0:1, 0:512])
            bcg = B[:, 2048:2560]
            bcv = B[:, 2560:3072]
            k.mm(P[3][0:128, 0:512], ones_r, rws[0:1, 0:512])
            k.ts("v", bcg, P[3][0:128, 0:512], gw_g, None, ALU.mult)
            k.ts("v", bcv, P[3][0:128, 0:512], gw_v, None, ALU.mult)
            zmx = B[:, 3072:3584]
            zmn = B[:, 3584:4096]
            zt = B[:, 4096:4608]
            for ht in range(64):
                for kk in range(16):
                    k.dma("q", wqbf[:, 0:128],
                          wqg_dram[kk * 128:kk * 128 + 128, ht * 128:ht * 128 + 128])
                    k.mm(P[0], wqbf[:, 0:128], BFq[:, kk * 512:kk * 512 + 512],
                         start=(kk == 0), stop=(kk == 15))
                    k.dma("q", zqbf[:, 0:128],
                          wqv_dram[kk * 128:kk * 128 + 128, ht * 128:ht * 128 + 128])
                    k.mm(P[1], zqbf[:, 0:128], BFq[:, kk * 512:kk * 512 + 512],
                         start=(kk == 0), stop=(kk == 15))
                k.tt("v", wsta, P[0], bcg, ALU.mult)
                k.ts("v", wsta, wsta, bg_sb[:, ht:ht + 1], None, ALU.add)
                k.act(wsta, wsta, ACT.Silu)
                k.tt("v", wstb, P[1], bcv, ALU.mult)
                k.ts("v", wstb, wstb, bv_sb[:, ht:ht + 1], None, ALU.add)
                k.tt("v", zt, wsta, wstb, ALU.mult)
                k.dma("gd", z_dram[ht], zt)
                if ht == 0:
                    k.cp("v", zmx, zt)
                    k.cp("v", zmn, zt)
                else:
                    k.tt("v", zmx, zmx, zt, ALU.max)
                    k.tt("v", zmn, zmn, zt, ALU.min)
            for jj in range(4):
                k.tp(P[2][:, 0:128], zmx[:, jj * 128:jj * 128 + 128], ident)
                k.cp("v", pts, P[2][:, 0:128])
                k.red("v", col["t0"], pts, ALU.max)
                k.tp(P[2][:, 0:128], zmn[:, jj * 128:jj * 128 + 128], ident)
                k.cp("v", pts, P[2][:, 0:128])
                k.red("v", col["t1"], pts, ALU.max, absolute=True)
                k.tt("v", col["t0"], col["t0"], col["t1"], ALU.max)
                k.ts("v", col["gz"], col["t0"], 1e-5, None, ALU.max)
                k.recip(col["sxz"], col["gz"])
                k.ts("v", col["sxz"], col["sxz"], 127.0, None, ALU.mult)
                k.ts("v", col["t1"], col["gz"], float(1.0 / 127.0), None, ALU.mult)
                k.tp(P[4][0:1, jj * 128:jj * 128 + 128], col["sxz"], ident)
                k.tp(P[5][0:1, jj * 128:jj * 128 + 128], col["t1"], ident)
            k.cp("v", rws[0:1, 0:512], P[4][0:1, 0:512])
            sxzbc = B[:, 4608:5120]
            k.mm(P[3][0:128, 0:512], ones_r, rws[0:1, 0:512])
            k.cp("v", sxzbc, P[3][0:128, 0:512])
            k.cp("v", rws[0:1, 512:1024], P[5][0:1, 0:512])
            bco = B[:, 5120:5632]
            k.mm(P[3][0:128, 0:512], ones_r, rws[0:1, 512:1024])
            k.ts("v", bco, P[3][0:128, 0:512], gw_o, None, ALU.mult)
            for mg in range(2):
                for kk in range(64):
                    k.dma("q", zt, z_dram[kk])
                    k.tt("v", qsc, zt, sxzbc, ALU.mult)
                    k.cp("v", isc, qsc)
                    k.cp("v", qsc, isc)
                    k.cp("v", zqbf, qsc)
                    for mi in range(8):
                        m = mg * 8 + mi
                        k.dma("q", wqbf[:, 0:128],
                              wqo_dram[kk * 128:kk * 128 + 128, m * 128:m * 128 + 128])
                        k.mm(P[mi], wqbf[:, 0:128], zqbf,
                             start=(kk == 0), stop=(kk == 63))
                for mi in range(8):
                    m = mg * 8 + mi
                    ct = C[:, m * 512:m * 512 + 512]
                    k.tt("v", ct, P[mi], bco, ALU.mult)
                    k.ts("v", ct, ct, bo_sb[:, m:m + 1], None, ALU.add)
            ln_step(th, x3_dram, out_o, build_xt=False)

    l0_attn()
    if STAGES not in ("l0a",):
        l0_ffn()
    if STAGES not in ("l0a", "l0"):
        l1_attn()
    if STAGES == "all":
        l1_ffn()

    for ln in range(4):
        if k.seq[f"gd{ln}"]:
            k.eng["g"].wait_ge(k.sem[f"gd{ln}"], k._cover(f"gd{ln}", k.seq[f"gd{ln}"]))
        if k.seq[f"q{ln}"]:
            k.eng["g"].wait_ge(k.sem[f"q{ln}"], k._cover(f"q{ln}", k.seq[f"q{ln}"]))

    k.ctx.close()
    return nc


def _prep_inputs(inputs):
    f = lambda a: np.ascontiguousarray(np.asarray(a, dtype=np.float32))
    x = np.asarray(inputs["L0_attn_x"]).reshape(S, D)
    wq = np.asarray(inputs["L1_attn_W_qkvz"])
    perm = np.zeros(12288, dtype=np.int64)
    for kh in range(16):
        base = 768 * kh
        perm[128 * kh:128 * kh + 128] = np.arange(base, base + 128)
        perm[2048 + 128 * kh:2048 + 128 * kh + 128] = np.arange(base + 128, base + 256)
        perm[4096 + 256 * kh:4096 + 256 * kh + 256] = np.arange(base + 256, base + 512)
        perm[8192 + 256 * kh:8192 + 256 * kh + 256] = np.arange(base + 512, base + 768)
    wqkvz = f(wq[:, perm])
    wba = np.asarray(inputs["L1_attn_W_ba"])
    bidx = np.array([4 * kh + r for kh in range(16) for r in (0, 1)])
    aidx = np.array([4 * kh + 2 + r for kh in range(16) for r in (0, 1)])
    cw = np.asarray(inputs["L1_attn_conv_weight"]).reshape(64, 128, 4)
    shared = {
        "wdq": f(inputs["L0_attn_W_dq"]), "wuq": f(inputs["L0_attn_W_uq"]),
        "wdkv": f(inputs["L0_attn_W_dkv"]), "wuk": f(inputs["L0_attn_W_uk"]),
        "wuv": f(inputs["L0_attn_W_uv"]), "wqr": f(inputs["L0_attn_W_qr"]),
        "wkr": f(inputs["L0_attn_W_kr"]), "wout0": f(inputs["L0_attn_W_out"]),
        "gq": f(np.asarray(inputs["L0_attn_g_q"]).reshape(4, 128).T),
        "gkv": f(np.asarray(inputs["L0_attn_g_kv"]).reshape(4, 128).T),
        "bout0": f(np.asarray(inputs["L0_attn_b_out"]).reshape(16, 128).T),
        "gwT": f(np.asarray(inputs["L0_ffn_gate_w"]).T),
        "efc1T": f(np.transpose(np.asarray(inputs["L0_ffn_expert_fc1"]), (0, 2, 1))),
        "efc2T": f(np.transpose(np.asarray(inputs["L0_ffn_expert_fc2"]), (0, 2, 1))),
        "eprojT": f(np.transpose(np.asarray(inputs["L0_ffn_expert_proj"]), (0, 2, 1))),
        "sfc1T": f(np.asarray(inputs["L0_ffn_shared_fc1"]).T),
        "sfc2T": f(np.asarray(inputs["L0_ffn_shared_fc2"]).T),
        "sprojT": f(np.asarray(inputs["L0_ffn_shared_proj"]).T),
        "sgwT": f(np.asarray(inputs["L0_ffn_shared_gate_w"]).T),
        "wqkvz": wqkvz,
        "wbab": f(wba[:, bidx]), "wbaa": f(wba[:, aidx]),
        "cwt": f(np.transpose(cw, (1, 0, 2)).reshape(128, 256)),
        "alog": f(np.asarray(inputs["L1_attn_A_log"]).reshape(32, 1)),
        "dtb": f(np.asarray(inputs["L1_attn_dt_bias"]).reshape(32, 1)),
        "nw": f(np.asarray(inputs["L1_attn_norm_weight"]).reshape(1, 128)),
        "wout1": f(inputs["L1_attn_W_out"]),
        "wgT": f(np.asarray(inputs["L1_ffn_gate_w"]).T),
        "wvT": f(np.asarray(inputs["L1_ffn_val_w"]).T),
        "woT": f(np.asarray(inputs["L1_ffn_out_w"]).T),
        "bg": f(np.asarray(inputs["L1_ffn_gate_b"]).reshape(64, 128).T),
        "bv": f(np.asarray(inputs["L1_ffn_val_b"]).reshape(64, 128).T),
        "bo": f(np.asarray(inputs["L1_ffn_out_b"]).reshape(16, 128).T),
    }
    in_maps = []
    for c in range(NCORES):
        m = dict(shared)
        m["x"] = f(x[c * TOK:(c + 1) * TOK, :])
        m["thr"] = np.arange(c * TOK, (c + 1) * TOK,
                             dtype=np.float32).reshape(TOK, 1).copy()
        oh = np.zeros((1, 8), np.float32); oh[0, c] = 1.0
        m["mcs"] = oh
        bd = np.zeros((1, 8), np.float32)
        if c > 0:
            bd[0, c - 1] = 1.0
        m["mbd"] = bd
        in_maps.append(m)
    return in_maps


def _run(inputs, taps=()):
    nc = build_program(tuple(taps))
    in_maps = _prep_inputs(inputs)
    res = run_bass_kernel_spmd(nc, in_maps, core_ids=list(range(NCORES)))
    outs = [np.asarray(r["o"]) for r in res.results]
    full = np.concatenate(outs, axis=0).reshape(1, S, D).astype(np.float32)
    tapvals = [{name: np.asarray(r["tap_" + name]) for name in taps}
               for r in res.results]
    return full, tapvals


def kernel(**inputs):
    out, _ = _run(inputs, taps=TAPS)
    return out


if __name__ == "__main__":
    nc = build_program(())
    print("built", sum(1 for _ in nc.all_instructions()), "instructions")


# version 10
# speedup vs baseline: 1.1769x; 1.1769x over previous
import contextlib

import numpy as np

import concourse.bass as bass
import concourse.mybir as mybir
from concourse.bass_utils import run_bass_kernel_spmd

F32 = mybir.dt.float32
BF16 = mybir.dt.bfloat16
I32 = mybir.dt.int32
ALU = mybir.AluOpType
ACT = mybir.ActivationFunctionType
AXL = mybir.AxisListType

NCORES = 8
S = 8192
D = 2048
TOK = 1024

NEG30 = -1.0e30
PEN = -1.0e27
RSQ = float(1.0 / np.sqrt(128.0))

TAPS = ()
STAGES = "all"


class K:

    def __init__(self, nc):
        self.nc = nc
        self.ctx = contextlib.ExitStack()
        keys = ["t", "v", "s", "g"]
        self.inc_amt = {"t": 1, "v": 1, "s": 1, "g": 1}
        for b in ("q", "gd"):
            for ln in range(4):
                keys.append(f"{b}{ln}")
                self.inc_amt[f"{b}{ln}"] = 16
        self.rot = {"q": 0, "gd": 0}
        self.sem = {k: self.ctx.enter_context(nc.semaphore("S" + k)) for k in keys}
        self.eng = {}
        self.exp = {k: 0 for k in keys}
        self.seq = {k: 0 for k in keys}
        self.last = {k: None for k in keys}
        self.incs = {k: [] for k in keys}
        self.wm = {}
        self.writer = {}
        self.readers = {}
        self.dirty = {k: set() for k in keys}

    def _cover(self, ek, seq):
        for val, sq in self.incs[ek]:
            if sq >= seq:
                return val
        self.exp[ek] += self.inc_amt[ek]
        self.last[ek].then_inc(self.sem[ek], self.inc_amt[ek])
        self.incs[ek].append((self.exp[ek], self.seq[ek]))
        return self.exp[ek]

    def _wait(self, cons, ek, val):
        key = (cons, ek)
        if self.wm.get(key, -1) >= val:
            return
        self.eng[cons].wait_ge(self.sem[ek], val)
        self.wm[key] = val

    def dep(self, key, names_in, names_out):
        need_drain = False
        for nm in names_in:
            w = self.writer.get(nm)
            if w is None:
                continue
            wek, wseq = w
            if wek == key:
                if nm in self.dirty[key] and key in ("v", "s", "g"):
                    need_drain = True
            else:
                self._wait(key, wek, self._cover(wek, wseq))
        for nm in names_out:
            for rek, rseq in list(self.readers.get(nm, {}).items()):
                if rek != key:
                    self._wait(key, rek, self._cover(rek, rseq))
            w = self.writer.get(nm)
            if w is not None and w[0] != key:
                self._wait(key, w[0], self._cover(w[0], w[1]))
        if need_drain:
            self.eng[key].drain()
            self.dirty[key].clear()

    def did(self, key, inst, names_in, names_out):
        self.seq[key] += 1
        self.last[key] = inst
        for nm in names_in:
            self.readers.setdefault(nm, {})[key] = self.seq[key]
        for nm in names_out:
            self.writer[nm] = (key, self.seq[key])
            self.readers[nm] = {}
            self.dirty[key].add(nm)

    def op(self, key, fn, ins, outs):
        ni = [a.tensor.name for a in ins]
        no = [a.tensor.name for a in outs]
        self.dep(key, ni, no)
        inst = fn(self.eng[key])
        self.did(key, inst, ni, no)
        return inst

    def dma(self, base, out, in_):
        ln = self.rot[base]
        self.rot[base] = (ln + 1) % 4
        key = f"{base}{ln}"
        inst = self.op(key, lambda e: e.dma_start(out=out, in_=in_), [in_], [out])
        self._cover(key, self.seq[key])
        return inst

    def mm(self, out, lhsT, rhs, start=True, stop=True):
        return self.op("t", lambda e: e.matmul(out, lhsT, rhs, start=start, stop=stop),
                       [lhsT, rhs], [out])

    def tp(self, out, in_, ident):
        p = in_.shape[0]
        idap = ident if p == ident.shape[0] else ident[0:p, 0:p]
        return self.op("t", lambda e: e.transpose(out, in_, idap), [in_, idap], [out])

    def cp(self, key, out, in_):
        return self.op(key, lambda e: e.tensor_copy(out, in_), [in_], [out])

    def tt(self, key, out, a, b, alu):
        return self.op(key, lambda e: e.tensor_tensor(out, a, b, alu), [a, b], [out])

    def ts(self, key, out, in_, s1, s2, op0, op1=ALU.bypass, accum=None):
        ins = [in_] + [x for x in (s1, s2) if isinstance(x, bass.AP)]
        outs = [out] + ([accum] if accum is not None else [])
        return self.op(key, lambda e: e.tensor_scalar(out, in_, s1, s2, op0, op1,
                                                      accum_out=accum), ins, outs)

    def stt(self, key, out, in0, scalar, in1, op0, op1):
        ins = [in0, in1] + ([scalar] if isinstance(scalar, bass.AP) else [])
        return self.op(key, lambda e: e.scalar_tensor_tensor(out, in0, scalar, in1,
                                                             op0, op1), ins, [out])

    def act(self, out, in_, func, bias=0.0, scale=1.0, accum=None):
        ins = [in_] + [x for x in (bias, scale) if isinstance(x, bass.AP)]
        outs = [out] + ([accum] if accum is not None else [])
        return self.op("s", lambda e: e.activation(out, in_, func, bias=bias,
                                                   scale=scale, accum_out=accum),
                       ins, outs)

    def recip(self, out, in_):
        return self.op("v", lambda e: e.reciprocal(out, in_), [in_], [out])

    def red(self, key, out, in_, alu, absolute=False, neg=False):
        return self.op(key, lambda e: e.tensor_reduce(out, in_, AXL.X, alu,
                                                      apply_absolute_value=absolute,
                                                      negate=neg), [in_], [out])

    def memset(self, key, ap, val):
        return self.op(key, lambda e: e.memset(ap, val), [], [ap])

    def iota(self, ap, pattern, base=0, mult=0):
        return self.op("g", lambda e: e.iota(ap, pattern=pattern, base=base,
                                             channel_multiplier=mult,
                                             allow_small_or_imprecise_dtypes=True),
                       [], [ap])

    def asel(self, out, in_, pattern, fill, base, mult):
        return self.op("g", lambda e: e.affine_select(out=out, in_=in_, pattern=pattern,
                                                      compare_op=ALU.is_ge, fill=fill,
                                                      base=base, channel_multiplier=mult),
                       [in_], [out])

    def coll(self, kind, ins, outs):
        def f(e):
            return e.collective_compute(kind, ALU.bypass,
                                        replica_groups=[list(range(NCORES))],
                                        ins=ins, outs=outs)
        inst = self.op("g", f, ins, outs)
        self._cover("g", self.seq["g"])
        return inst


def build_program(taps):
    nc = bass.Bass("TRN2", target_bir_lowering=False)
    k = K(nc)

    def din(name, shape, dtype=F32):
        return nc.dram_tensor(name, shape, dtype, kind="ExternalInput").ap()

    def dout(name, shape, dtype=F32):
        return nc.dram_tensor(name, shape, dtype, kind="ExternalOutput").ap()

    def dtmp(name, shape, dtype=F32):
        return nc.dram_tensor(name, shape, dtype, kind="Internal").ap()

    def sb(name, shape, dtype=F32):
        return nc.alloc_sbuf_tensor(name, shape, dtype).ap()

    def ps(name):
        return nc.alloc_psum_tensor(name, [128, 512], F32).ap()

    x0 = din("x", [TOK, D])
    thr = din("thr", [TOK, 1])
    mcs = din("mcs", [1, 8])
    mbd = din("mbd", [1, 8])
    wdq = din("wdq", [D, 512]); wuq = din("wuq", [512, 1024])
    wdkv = din("wdkv", [D, 512]); wuk = din("wuk", [512, 1024])
    wuv = din("wuv", [512, D]); wqr = din("wqr", [512, 1024])
    wkr = din("wkr", [D, 1024]); wout0 = din("wout0", [D, D])
    gq = din("gq", [128, 4]); gkv = din("gkv", [128, 4]); bout0 = din("bout0", [128, 16])
    gwT = din("gwT", [D, 8])
    efc1T = din("efc1T", [8, D, 1408]); efc2T = din("efc2T", [8, D, 1408])
    eprojT = din("eprojT", [8, 1408, D])
    sfc1T = din("sfc1T", [D, 5632]); sfc2T = din("sfc2T", [D, 5632])
    sprojT = din("sprojT", [5632, D]); sgwT = din("sgwT", [D, 1])
    wqkvz = din("wqkvz", [D, 12288])
    wbab = din("wbab", [D, 32]); wbaa = din("wbaa", [D, 32])
    cwt_in = din("cwt", [128, 256])
    alog = din("alog", [32, 1]); dtb = din("dtb", [32, 1]); nw_in = din("nw", [1, 128])
    wout1 = din("wout1", [4096, D])
    wgT = din("wgT", [D, 8192]); wvT = din("wvT", [D, 8192]); woT = din("woT", [8192, D])
    bg = din("bg", [128, 64]); bv = din("bv", [128, 64]); bo = din("bo", [128, 16])

    out_o = dout("o", [TOK, D])
    tap_t = {}
    tshapes = {"x1": [TOK, D], "x2": [TOK, D], "x3": [TOK, D],
               "cq": [128, 4096], "qkvc": [64, 128, 1024], "s0": [128, 4096],
               "csrep": [128, 1024], "comb": [8, 1024], "ao": [16, 128, 1024],
               "dl0": [128, 8192], "dl1": [128, 8192]}
    for name in taps:
        tap_t[name] = dout("tap_" + name, tshapes[name])

    cs_dram = dtmp("cs_dram", [32, 1024])
    gt_dram = dtmp("gt_dram", [32, 1024])
    qt_dram = dtmp("qt_dram", [16, 128, 1024])
    kag_in = dtmp("kag_in", [16, 128, 1024])
    kag_out = dtmp("kag_out", [8, 16, 128, 1024])
    vag_in = dtmp("vag_in", [1024, D])
    vag_out = dtmp("vag_out", [8, 1024, D])
    attn_o = dtmp("attn_o", [16, 128, 1024])
    x1_dram = dtmp("x1_dram", [TOK, D])
    x2_dram = dtmp("x2_dram", [TOK, D])
    x3_dram = dtmp("x3_dram", [TOK, D])
    comb_dram = dtmp("comb_dram", [8, 1024])
    mixed_dram = dtmp("mixed_dram", [64, 128, 1024])
    zT_dram = dtmp("zT_dram", [32, 128, 1024])
    bnd_in = dtmp("bnd_in", [128, 192])
    bnd_out = dtmp("bnd_out", [8, 128, 192])
    qkvc_dram = dtmp("qkvc_dram", [64, 128, 1024])
    kn_dram = dtmp("kn_dram", [8, 32, 128, 128])
    xi_dram = dtmp("xi_dram", [8, 32, 128, 128])
    rl_dram = dtmp("rl_dram", [8, 32, 128, 128])
    mc_dram = dtmp("mc_dram", [8, 32, 128, 128])
    sl_dram = dtmp("sl_dram", [8, 32, 128, 128])
    sp_in = dtmp("sp_in", [2, 32, 128, 128])
    sp_out = dtmp("sp_out", [8, 2, 32, 128, 128])
    z_dram = dtmp("z_dram", [64, 128, 512])
    wqg_dram = dtmp("wqg_dram", [D, 8192], BF16)
    wqv_dram = dtmp("wqv_dram", [D, 8192], BF16)
    wqo_dram = dtmp("wqo_dram", [8192, D], BF16)

    A0 = sb("A0", [128, 8192])
    A1 = sb("A1", [128, 8192])
    B0 = sb("B0", [128, 8192])
    B1 = sb("B1", [128, 4096])
    C = sb("C", [128, 8192])
    ident = sb("ident", [128, 128])
    identb = sb("identb", [128, 128], BF16)
    lowneg = sb("lowneg", [128, 128])
    upneg = sb("upneg", [128, 128])
    trilinc = sb("trilinc", [128, 128])
    kidx = sb("kidx", [128, 512])
    csrep = sb("csrep", [128, 1024])
    nwbc = sb("nwbc", [128, 128])
    ones_r = sb("ones_r", [1, 128])
    ones_c = sb("ones_c", [128, 1])
    c127 = sb("c127", [128, 1])
    ceps5 = sb("ceps5", [128, 1])
    ceps6 = sb("ceps6", [128, 1])
    gq_sb = sb("gq_sb", [128, 4]); gkv_sb = sb("gkv_sb", [128, 4])
    bout0_sb = sb("bout0_sb", [128, 16])
    bg_sb = sb("bg_sb", [128, 64]); bv_sb = sb("bv_sb", [128, 64])
    bo_sb = sb("bo_sb", [128, 16])
    bcm_cs = sb("bcm_cs", [128, 8])
    bcm_bd = sb("bcm_bd", [128, 8])
    thr_sb = sb("thr_sb", [128, 8])
    wsta = sb("wsta", [128, 512]); wstb = sb("wstb", [128, 512])
    wring = None
    wqbf = sb("wqbf", [128, 512], BF16)
    isc = sb("isc", [128, 512], I32)
    qsc = sb("qsc", [128, 512])
    zqbf = sb("zqbf", [128, 512], BF16)
    BFq = sb("BFq", [128, 8192], BF16)
    gT = sb("gT", [32, 1024])
    bT = sb("bT", [32, 1024])
    GT = sb("GT", [32, 1024])
    G_tm = sb("G_tm", [128, 256])
    b_tm = sb("b_tm", [128, 256])
    cwt = sb("cwt_sb", [128, 256])
    rws = sb("rws", [1, 1024])
    shg_sb = sb("shg_sb", [1, 1024])
    crow = None
    nw_sb = sb("nw_sb", [1, 128])
    ost = sb("ost", [128, 512])
    pts = ost[:, 256:384]
    gw_g = sb("gw_g", [128, 1]); gw_v = sb("gw_v", [128, 1]); gw_o = sb("gw_o", [128, 1])
    alog_sb = sb("alog_sb", [32, 1])
    dtb_sb = sb("dtb_sb", [32, 1])
    thp = sb("thp", [32, 1]); thc = sb("thc", [32, 1]); ths = sb("ths", [32, 1])
    hpi = sb("hpi", [32, 1])
    W0 = None
    col = {nm: sb("col_" + nm, [128, 1]) for nm in
           ["negmax", "rsum", "gx", "sx", "gz", "sxz", "t0", "t1",
            "gcb", "eg", "dec", "egc", "ssq", "rstd", "nexa", "m1", "m2"]}

    P = [ps(f"P{i}") for i in range(8)]
    crow = rws
    W0 = [C[:, i * 128:(i + 1) * 128] for i in range(22)]
    wstc = B1[:, 3072:3584]
    wring = [wsta, wstb, qsc, wstc]
    bndb = C[:, 2112:2304]
    bndl = C[:, 2304:2496]

    for ln in range(4):
        k.eng[f"q{ln}"] = nc.sync
        k.eng[f"gd{ln}"] = nc.gpsimd
    k.eng["g"] = nc.gpsimd
    k.eng["v"] = nc.vector
    k.eng["s"] = nc.scalar
    k.eng["t"] = nc.tensor

    def bcast_row(row_ap, n, dst_sb, pbank):
        for j0 in range(0, n, 512):
            w = min(512, n - j0)
            k.mm(pbank[0:128, j0 % 512:j0 % 512 + w], ones_r, row_ap[0:1, j0:j0 + w])
            k.cp("v", dst_sb[:, j0:j0 + w], pbank[0:128, j0 % 512:j0 % 512 + w])

    def bcast_col(src11, dst_col, pbank):
        k.mm(pbank[0:128, 0:1], ones_r, src11)
        k.cp("v", dst_col, pbank[0:128, 0:1])

    def tp128(dst_sb, src_sb, pb, bf=False):
        k.tp(pb, src_sb, identb if bf else ident)
        k.cp("v", dst_sb, pb)

    def xt_slice(kk, th):
        buf = A0 if th == 0 else A1
        return buf[:, kk * 512: kk * 512 + 512]

    k.memset("v", ones_r, 1.0)
    k.memset("v", ones_c, 1.0)
    k.memset("v", c127, 127.0)
    k.memset("v", ceps5, 1e-5)
    k.memset("v", ceps6, 1e-6)
    k.memset("g", ident, 1.0)
    k.asel(ident, ident, [[1, 128]], 0.0, 0, -1)
    k.asel(ident, ident, [[-1, 128]], 0.0, 0, 1)
    k.cp("v", identb, ident)
    k.memset("g", lowneg, 0.0)
    k.asel(lowneg, lowneg, [[1, 128]], NEG30, -1, -1)
    k.memset("g", upneg, 0.0)
    k.asel(upneg, upneg, [[-1, 128]], NEG30, 0, 1)
    k.memset("g", trilinc, 1.0)
    k.asel(trilinc, trilinc, [[1, 128]], 0.0, 0, -1)
    k.iota(kidx, [[1, 512]], 0, 0)
    k.dma("q", cwt, cwt_in)
    k.dma("q", nw_sb[0:1, 0:8], mcs)
    k.dma("q", nw_sb[0:1, 8:16], mbd)
    k.dma("q", alog_sb, alog)
    k.dma("q", dtb_sb, dtb)
    for qt in range(8):
        k.dma("q", thr_sb[:, qt:qt + 1], thr[qt * 128:qt * 128 + 128, :])
    for si in range(8):
        k.mm(P[3][0:128, si:si + 1], ones_r, nw_sb[0:1, si:si + 1])
        k.mm(P[4][0:128, si:si + 1], ones_r, nw_sb[0:1, 8 + si:9 + si])
    k.cp("v", bcm_cs, P[3][0:128, 0:8])
    k.cp("v", bcm_bd, P[4][0:128, 0:8])
    k.dma("q", nw_sb, nw_in)
    bcast_row(nw_sb, 128, nwbc, P[3])
    k.dma("q", gq_sb, gq)
    k.dma("q", gkv_sb, gkv)
    k.dma("q", bout0_sb, bout0)
    k.dma("q", bg_sb, bg)
    k.dma("q", bv_sb, bv)
    k.dma("q", bo_sb, bo)

    csC = A0[0:32, 0:8192]
    csS = A1[0:32, 0:8192]
    tmpa = B0[0:32, 0:4096]
    tmpb = B0[0:32, 4096:8192]
    cs_own = B1[0:32, 0:1024]
    k.iota(thp, [[0, 1]], 0, 1)
    k.act(thp, thp, ACT.Exp, scale=float(-np.log(10000.0) / 64.0))
    k.memset("v", hpi, float(np.pi / 2))
    k.act(thc, thp, ACT.Sin, bias=hpi)
    k.act(ths, thp, ACT.Sin)
    k.memset("v", csC[:, 0:1], 1.0)
    k.memset("v", csS[:, 0:1], 0.0)
    k.cp("v", csC[:, 1:2], thc)
    k.cp("v", csS[:, 1:2], ths)
    n = 1
    while n < S - 1:
        m = min(n, S - 1 - n)
        cn = csC[:, n:n + 1]
        sn = csS[:, n:n + 1]
        k.ts("v", tmpa[:, 0:m], csC[:, 1:1 + m], cn, None, ALU.mult)
        k.ts("v", tmpb[:, 0:m], csS[:, 1:1 + m], sn, None, ALU.mult)
        k.tt("v", csC[:, n + 1:n + 1 + m], tmpa[:, 0:m], tmpb[:, 0:m], ALU.subtract)
        k.ts("v", tmpa[:, 0:m], csS[:, 1:1 + m], cn, None, ALU.mult)
        k.ts("v", tmpb[:, 0:m], csC[:, 1:1 + m], sn, None, ALU.mult)
        k.tt("v", csS[:, n + 1:n + 1 + m], tmpa[:, 0:m], tmpb[:, 0:m], ALU.add)
        n *= 2
    k.tt("v", csC, csC, csS, ALU.add)
    k.memset("v", cs_own, 0.0)
    for si in range(8):
        k.stt("v", cs_own, csC[:, 1024 * si:1024 * si + 1024],
              bcm_cs[0:32, si:si + 1], cs_own, ALU.mult, ALU.add)
    k.dma("gd", cs_dram, cs_own)
    for r in range(4):
        k.dma("q", csrep[32 * r:32 * r + 32, :], cs_dram)
    if "csrep" in tap_t:
        k.dma("gd", tap_t["csrep"], csrep)

    def proj(wap, kdim, mdim, rhs_fn, evac, mgroup=4):
        for m0 in range(0, mdim, mgroup):
            g = min(mgroup, mdim - m0)
            for kk in range(kdim):
                wst = wring[kk % 4]
                k.dma("q", wst[:, 0:g * 128],
                      wap[kk * 128:kk * 128 + 128, m0 * 128:m0 * 128 + g * 128])
                for mi in range(g):
                    for th in range(2):
                        k.mm(P[mi * 2 + th], wst[:, mi * 128:mi * 128 + 128],
                             rhs_fn(kk, th), start=(kk == 0), stop=(kk == kdim - 1))
            for mi in range(g):
                for th in range(2):
                    evac(m0 + mi, th, P[mi * 2 + th])

    def ln_step(th, x_src, x_dst, build_xt):
        for jj in range(4):
            j = th * 4 + jj
            xld = B1[:, 0:2048]
            k.dma("q", xld, x_src[j * 128:j * 128 + 128, :])
            for m in range(16):
                sl = (m % 4) * 128
                k.tp(P[2][:, sl:sl + 128],
                     C[:, m * 512 + jj * 128:m * 512 + jj * 128 + 128], ident)
                k.cp("v", wsta[:, sl:sl + 128], P[2][:, sl:sl + 128])
                if m % 4 == 3:
                    mm0 = (m // 4) * 512
                    k.tt("v", xld[:, mm0:mm0 + 512], xld[:, mm0:mm0 + 512],
                         wsta, ALU.add)
            k.red("v", col["t0"], xld, ALU.add, neg=True)
            k.ts("v", col["t1"], col["t0"], float(1.0 / 2048.0), None, ALU.mult)
            k.ts("v", xld, xld, col["t1"], None, ALU.add)
            k.act(B0[:, 2048:4096], xld, ACT.Square, accum=col["ssq"])
            k.act(col["rstd"], col["ssq"], ACT.Sqrt, bias=ceps5, scale=float(1.0 / 2048.0))
            k.recip(col["rstd"], col["rstd"])
            k.ts("v", xld, xld, col["rstd"], None, ALU.mult)
            k.dma("gd", x_dst[j * 128:j * 128 + 128, :], xld)
            if build_xt:
                for kk in range(16):
                    sl = (kk % 4) * 128
                    k.tp(P[2][:, sl:sl + 128], xld[:, kk * 128:kk * 128 + 128], ident)
                    k.cp("v", (A0 if th == 0 else A1)[:, kk * 512 + jj * 128:
                                kk * 512 + jj * 128 + 128],
                         P[2][:, sl:sl + 128])

    def l0_attn():
        for j in range(8):
            th, jj = j // 4, j % 4
            xld = B1[:, 0:2048]
            k.dma("q", xld, x0[j * 128:j * 128 + 128, :])
            for kk in range(16):
                sl = (kk % 4) * 128
                k.tp(P[2][:, sl:sl + 128], xld[:, kk * 128:kk * 128 + 128], ident)
                k.cp("v", (A0 if th == 0 else A1)[:, kk * 512 + jj * 128:
                            kk * 512 + jj * 128 + 128],
                     P[2][:, sl:sl + 128])

        def c_ev(base):
            def ev(m, th, pb):
                k.cp("v", B0[:, base + th * 2048 + m * 512:
                             base + th * 2048 + m * 512 + 512], pb)
            return ev

        def rms_apply(base, gvec):
            for th in range(2):
                pb = P[4] if th == 0 else P[5]
                for m in range(4):
                    t = B0[:, base + th * 2048 + m * 512: base + th * 2048 + m * 512 + 512]
                    k.act(wsta, t, ACT.Square)
                    k.mm(pb[0:1, 0:512], ones_c, wsta,
                         start=(m == 0), stop=(m == 3))
                k.act(rws[0:1, th * 512:th * 512 + 512], pb[0:1, 0:512], ACT.Sqrt,
                      bias=ceps5[0:1, :], scale=float(1.0 / 512.0))
                k.recip(rws[0:1, th * 512:th * 512 + 512],
                        rws[0:1, th * 512:th * 512 + 512])
            bcast_row(rws, 1024, C[:, 1024:2048], P[5])
            for th in range(2):
                for m in range(4):
                    t = B0[:, base + th * 2048 + m * 512: base + th * 2048 + m * 512 + 512]
                    k.tt("v", t, t, C[:, 1024 + th * 512:1024 + th * 512 + 512],
                         ALU.mult)
                    k.ts("v", t, t, gvec[:, m:m + 1], None, ALU.mult)

        def cq_slice(kk, th):
            return B0[:, th * 2048 + kk * 512: th * 2048 + kk * 512 + 512]

        def ckv_slice(kk, th):
            return B0[:, 4096 + th * 2048 + kk * 512: 4096 + th * 2048 + kk * 512 + 512]

        proj(wdq, 16, 4, xt_slice, c_ev(0))
        rms_apply(0, gq_sb)
        proj(wdkv, 16, 4, xt_slice, c_ev(4096))
        rms_apply(4096, gkv_sb)
        if "cq" in tap_t:
            for th in range(2):
                for m in range(4):
                    k.dma("gd", tap_t["cq"][:, th * 2048 + m * 512:
                                            th * 2048 + m * 512 + 512],
                          cq_slice(m, th))

        def qb_ev(m, th, pb):
            k.ts("v", ost, pb, RSQ, None, ALU.mult)
            k.dma("gd", qt_dram[2 * m, 0:64, th * 512:th * 512 + 512], ost[0:64, 0:512])
            k.dma("gd", qt_dram[2 * m + 1, 0:64, th * 512:th * 512 + 512],
                  ost[64:128, 0:512])

        def qr_ev(m, th, pb):
            k.stt("v", ost, pb, RSQ, csrep[:, th * 512:th * 512 + 512],
                  ALU.mult, ALU.mult)
            k.dma("gd", qt_dram[2 * m, 64:128, th * 512:th * 512 + 512], ost[0:64, 0:512])
            k.dma("gd", qt_dram[2 * m + 1, 64:128, th * 512:th * 512 + 512],
                  ost[64:128, 0:512])

        def kb_ev(m, th, pb):
            k.cp("v", ost, pb)
            k.dma("gd", kag_in[2 * m, 0:64, th * 512:th * 512 + 512], ost[0:64, 0:512])
            k.dma("gd", kag_in[2 * m + 1, 0:64, th * 512:th * 512 + 512],
                  ost[64:128, 0:512])

        def kr_ev(m, th, pb):
            k.tt("v", ost, pb, csrep[:, th * 512:th * 512 + 512], ALU.mult)
            k.dma("gd", kag_in[2 * m, 64:128, th * 512:th * 512 + 512], ost[0:64, 0:512])
            k.dma("gd", kag_in[2 * m + 1, 64:128, th * 512:th * 512 + 512],
                  ost[64:128, 0:512])

        def v_ev(m, th, pb):
            k.cp("v", wsta, pb)
            for j in range(4):
                k.tp(P[6][:, j * 128:j * 128 + 128], wsta[:, j * 128:j * 128 + 128],
                     ident)
            k.cp("v", wstb, P[6])
            for j in range(4):
                k.dma("gd", vag_in[th * 512 + j * 128: th * 512 + j * 128 + 128,
                                   m * 128:m * 128 + 128], wstb[:, j * 128:j * 128 + 128])

        proj(wuq, 4, 8, cq_slice, qb_ev)
        proj(wqr, 4, 8, cq_slice, qr_ev)
        proj(wuk, 4, 8, ckv_slice, kb_ev)
        proj(wkr, 16, 8, xt_slice, kr_ev)
        proj(wuv, 4, 16, ckv_slice, v_ev, mgroup=3)

        k.coll("AllGather", [kag_in], [kag_out])
        k.coll("AllGather", [vag_in], [vag_out])

        SS = A1[:, 0:8192]
        for h in range(16):
            for si in range(8):
                k.dma("q", B0[:, si * 1024:si * 1024 + 1024], kag_out[si, h])
            for kb in range(64):
                si, t0 = kb // 8, (kb % 8) * 128
                k.dma("q", A0[:, kb * 128:kb * 128 + 128],
                      vag_out[si, t0:t0 + 128, h * 128:h * 128 + 128])
            k.dma("q", C[:, 0:1024], qt_dram[h])
            for qt in range(8):
                for kc in range(16):
                    pb = P[0] if kc % 2 == 0 else P[1]
                    k.mm(pb, C[:, qt * 128:qt * 128 + 128],
                         B0[:, kc * 512:kc * 512 + 512])
                    k.ts("v", col["t1"], thr_sb[:, qt:qt + 1], float(-512.0 * kc),
                         None, ALU.add)
                    k.ts("v", wsta, kidx, col["t1"], 0.0, ALU.subtract, ALU.max)
                    k.stt("v", SS[:, kc * 512:kc * 512 + 512], wsta, PEN, pb,
                          ALU.mult, ALU.add)
                k.red("v", col["negmax"], SS, ALU.max, neg=True)
                k.act(SS, SS, ACT.Exp, bias=col["negmax"], accum=col["rsum"])
                k.recip(col["rsum"], col["rsum"])
                k.tp(P[4][0:1, 0:128], col["rsum"], ident)
                k.cp("v", rws[0:1, 0:128], P[4][0:1, 0:128])
                k.mm(P[3][0:128, 0:128], ones_r, rws[0:1, 0:128])
                k.cp("v", ost[:, 128:256], P[3][0:128, 0:128])
                pts4 = B1[:, 0:512]
                for kb4 in range(16):
                    for jb in range(4):
                        kb = kb4 * 4 + jb
                        k.tp(P[2][:, jb * 128:jb * 128 + 128],
                             SS[:, kb * 128:kb * 128 + 128], ident)
                    k.cp("v", pts4, P[2])
                    for jb in range(4):
                        kb = kb4 * 4 + jb
                        k.mm(P[5][:, 0:128], A0[:, kb * 128:kb * 128 + 128],
                             pts4[:, jb * 128:jb * 128 + 128],
                             start=(kb == 0), stop=(kb == 63))
                k.tt("v", ost[:, 0:128], P[5][:, 0:128], ost[:, 128:256], ALU.mult)
                k.dma("gd", attn_o[h][:, qt * 128:qt * 128 + 128], ost[:, 0:128])

        if "ao" in tap_t:
            for h in range(16):
                k.dma("gd", tap_t["ao"][h], attn_o[h])
        for h in range(16):
            for th in range(2):
                k.dma("q", (A0 if th == 0 else A1)[:, h * 512:h * 512 + 512],
                      attn_o[h][:, th * 512:th * 512 + 512])
        for th in range(2):
            for m0 in range(0, 16, 4):
                for kk in range(16):
                    wst = wring[kk % 4]
                    k.dma("q", wst, wout0[kk * 128:kk * 128 + 128,
                                          m0 * 128:m0 * 128 + 512])
                    for mi in range(4):
                        k.mm(P[mi], wst[:, mi * 128:mi * 128 + 128],
                             (A0 if th == 0 else A1)[:, kk * 512:kk * 512 + 512],
                             start=(kk == 0), stop=(kk == 15))
                for mi in range(4):
                    m = m0 + mi
                    k.ts("v", C[:, m * 512:m * 512 + 512], P[mi],
                         bout0_sb[:, m:m + 1], None, ALU.add)
            if f"dl{th}" in tap_t:
                k.dma("gd", tap_t[f"dl{th}"], C)
            ln_step(th, x0, x1_dram, build_xt=True)
        if "x1" in tap_t:
            k.dma("gd", tap_t["x1"], x1_dram)

    def l0_ffn():
        rtr = B0[:, 6144:7168]
        for th in range(2):
            pb = P[4] if th == 0 else P[5]
            for kk in range(16):
                wst = wsta if kk % 2 == 0 else wstb
                k.dma("q", wst[:, 0:8], gwT[kk * 128:kk * 128 + 128, :])
                k.mm(pb[0:8, 0:512], wst[:, 0:8],
                     xt_slice(kk, th), start=(kk == 0), stop=(kk == 15))
            k.cp("v", rtr[0:8, th * 512:th * 512 + 512], pb[0:8, 0:512])
        ctm = B0[:, 7168:7176]
        cT = B1[:, 2048:3072]
        for j in range(8):
            k.tp(P[2][0:128, 0:8], rtr[0:8, j * 128:j * 128 + 128], ident)
            k.cp("v", ctm, P[2][0:128, 0:8])
            k.red("v", col["t0"], ctm, ALU.max, neg=True)
            k.act(ctm, ctm, ACT.Exp, bias=col["t0"], accum=col["rsum"])
            k.recip(col["rsum"], col["rsum"])
            k.ts("v", ctm, ctm, col["rsum"], None, ALU.mult)
            k.red("v", col["m1"], ctm, ALU.max)
            k.ts("v", wsta[:, 0:8], ctm, col["m1"], None, ALU.is_ge)
            k.stt("v", wsta[:, 8:16], wsta[:, 0:8], NEG30, ctm, ALU.mult, ALU.add)
            k.red("v", col["m2"], wsta[:, 8:16], ALU.max)
            k.ts("v", wsta[:, 0:8], ctm, col["m2"], None, ALU.is_ge)
            k.tt("v", ctm, ctm, wsta[:, 0:8], ALU.mult)
            k.tp(P[2][0:8, 0:128], ctm, ident)
            k.cp("v", cT[0:8, j * 128:j * 128 + 128], P[2][0:8, 0:128])
        k.dma("gd", comb_dram, cT[0:8, :])
        if "comb" in tap_t:
            k.dma("gd", tap_t["comb"], comb_dram)

        for th in range(2):
            pb = P[4] if th == 0 else P[5]
            for kk in range(16):
                wst = wsta if kk % 2 == 0 else wstb
                k.dma("q", wst[:, 0:1], sgwT[kk * 128:kk * 128 + 128, :])
                k.mm(pb[0:1, 0:512], wst[:, 0:1],
                     xt_slice(kk, th), start=(kk == 0), stop=(kk == 15))
            k.act(shg_sb[0:1, th * 512:th * 512 + 512], pb[0:1, 0:512], ACT.Sigmoid)

        for th in range(2):
            first = [True]

            def ffn_branch(wf1, wf2, wpj, nf, gate_row):
                bcast_row(gate_row[0:1, th * 512:th * 512 + 512], 512,
                          B0[:, 5632:6144], P[3])
                for g0 in range(0, nf, 11):
                    gl = min(11, nf - g0)
                    for ft in range(gl):
                        f = g0 + ft
                        for kk in range(16):
                            wst = wring[kk % 4]
                            k.dma("q", wst[:, 0:128],
                                  wf1[kk * 128:kk * 128 + 128, f * 128:f * 128 + 128])
                            k.dma("q", wst[:, 128:256],
                                  wf2[kk * 128:kk * 128 + 128, f * 128:f * 128 + 128])
                            k.mm(P[0], wst[:, 0:128], xt_slice(kk, th),
                                 start=(kk == 0), stop=(kk == 15))
                            k.mm(P[1], wst[:, 128:256], xt_slice(kk, th),
                                 start=(kk == 0), stop=(kk == 15))
                        Ht = B0[:, ft * 512:ft * 512 + 512]
                        k.act(Ht, P[0], ACT.Silu)
                        k.tt("v", Ht, Ht, P[1], ALU.mult)
                        k.tt("v", Ht, Ht, B0[:, 5632:6144], ALU.mult)
                    for m0 in range(0, 16, 4):
                        for ft in range(gl):
                            f = g0 + ft
                            wst = wring[ft % 4]
                            k.dma("q", wst, wpj[f * 128:f * 128 + 128,
                                                m0 * 128:m0 * 128 + 512])
                            for mi in range(4):
                                k.mm(P[mi], wst[:, mi * 128:mi * 128 + 128],
                                     B0[:, ft * 512:ft * 512 + 512],
                                     start=(ft == 0), stop=(ft == gl - 1))
                        for mi in range(4):
                            yt = C[:, (m0 + mi) * 512:(m0 + mi) * 512 + 512]
                            if first[0]:
                                k.cp("v", yt, P[mi])
                            else:
                                k.tt("v", yt, yt, P[mi], ALU.add)
                    first[0] = False

            for e in range(8):
                k.dma("q", crow, comb_dram[e:e + 1, :])
                ffn_branch(efc1T[e], efc2T[e], eprojT[e], 11, crow)
            ffn_branch(sfc1T, sfc2T, sprojT, 44, shg_sb)
            ln_step(th, x1_dram, x2_dram, build_xt=True)
        if "x2" in tap_t:
            k.dma("gd", tap_t["x2"], x2_dram)

    def l1_attn():
        for nm, wap, dst in (("b", wbab, bT), ("a", wbaa, gT)):
            for th in range(2):
                pb = P[4] if th == 0 else P[5]
                for kk in range(16):
                    wst = wsta if kk % 2 == 0 else wstb
                    k.dma("q", wst[:, 0:32], wap[kk * 128:kk * 128 + 128, :])
                    k.mm(pb[0:32, 0:512], wst[:, 0:32],
                         xt_slice(kk, th), start=(kk == 0), stop=(kk == 15))
                if nm == "b":
                    k.act(dst[0:32, th * 512:th * 512 + 512], pb[0:32, 0:512],
                          ACT.Sigmoid)
                else:
                    sl = dst[0:32, th * 512:th * 512 + 512]
                    k.act(sl, pb[0:32, 0:512], ACT.Exp, bias=dtb_sb)
                    k.act(sl, sl, ACT.Ln, bias=ones_c[0:32, :])
        k.act(col["nexa"][0:32, :], alog_sb, ACT.Exp)
        k.ts("v", col["nexa"][0:32, :], col["nexa"][0:32, :], -1.0, None, ALU.mult)
        k.ts("v", gT, gT, col["nexa"][0:32, :], None, ALU.mult)
        for c in range(8):
            k.tp(P[2][0:128, 0:32], gT[0:32, c * 128:c * 128 + 128], ident)
            k.cp("v", wsta[:, 0:32], P[2][0:128, 0:32])
            k.mm(P[4][0:128, 0:32], trilinc, wsta[:, 0:32])
            k.cp("v", G_tm[:, c * 32:c * 32 + 32], P[4][0:128, 0:32])
            k.tp(P[2][0:32, 0:128], G_tm[:, c * 32:c * 32 + 32], ident)
            k.cp("v", GT[0:32, c * 128:c * 128 + 128], P[2][0:32, 0:128])
            k.tp(P[2][0:128, 0:32], bT[0:32, c * 128:c * 128 + 128], ident)
            k.cp("v", b_tm[:, c * 32:c * 32 + 32], P[2][0:128, 0:32])
        k.dma("gd", gt_dram, GT)

        def qkv_ev(m, th, pb):
            k.cp("v", wstb, pb)
            k.dma("gd", mixed_dram[m][:, th * 512:th * 512 + 512], wstb)
            if th == 1:
                k.dma("gd", bnd_in[:, m * 3:m * 3 + 3], wstb[:, 509:512])

        def z_ev(m, th, pb):
            k.act(wstb, pb, ACT.Silu)
            k.dma("gd", zT_dram[m][:, th * 512:th * 512 + 512], wstb)

        proj(wqkvz[:, 0:8192], 16, 64, xt_slice, qkv_ev)
        proj(wqkvz[:, 8192:12288], 16, 32, xt_slice, z_ev)

        k.coll("AllGather", [bnd_in], [bnd_out])
        k.memset("v", bndb, 0.0)
        for si in range(8):
            k.dma("q", bndl, bnd_out[si])
            k.stt("v", bndb, bndl, bcm_bd[:, si:si + 1], bndb, ALU.mult, ALU.add)

        cbuf = C[:, 0:1027]
        cvo = C[:, 1040:2064]
        for ft in range(64):
            k.dma("q", cbuf[:, 3:1027], mixed_dram[ft])
            k.cp("v", cbuf[:, 0:3], bndb[:, ft * 3:ft * 3 + 3])
            k.ts("v", cvo, cbuf[:, 0:1024], cwt[:, ft * 4:ft * 4 + 1], None, ALU.mult)
            for j in range(1, 4):
                k.stt("v", cvo, cbuf[:, j:j + 1024], cwt[:, ft * 4 + j:ft * 4 + j + 1],
                      cvo, ALU.mult, ALU.add)
            k.act(cvo, cvo, ACT.Silu)
            k.dma("gd", qkvc_dram[ft], cvo)
        if "qkvc" in tap_t:
            for ft in range(64):
                k.dma("gd", tap_t["qkvc"][ft], qkvc_dram[ft])

        (kTr, k_tm, v_tm, kT_n, d_sb, e_sb, uoff, ut_sb, x_sb, xt_sb, z_sb,
         rl_sb, ek_sb, w2_sb, kd_sb, sl_sb, t2_sb, mc_sb, dmp, q_tm, qT_n,
         rh_sb) = W0
        spanM = B0[:, 0:4096]
        spanS = B0[:, 4096:8192]

        for c in range(8):
            for h in range(32):
                gcol = G_tm[:, c * 32 + h:c * 32 + h + 1]
                grow = rws[0:1, 0:128]
                k.dma("q", grow, gt_dram[h:h + 1, c * 128:c * 128 + 128])
                bcol = b_tm[:, c * 32 + h:c * 32 + h + 1]
                k.dma("q", kTr, qkvc_dram[16 + h // 2][:, c * 128:c * 128 + 128])
                k.dma("q", d_sb, qkvc_dram[32 + h][:, c * 128:c * 128 + 128])
                tp128(k_tm, kTr, P[2][:, 0:128])
                k.act(dmp, k_tm, ACT.Square, accum=col["ssq"])
                k.act(col["rstd"], col["ssq"], ACT.Sqrt, bias=ceps6)
                k.recip(col["rstd"], col["rstd"])
                k.ts("v", k_tm, k_tm, col["rstd"], None, ALU.mult)
                tp128(kT_n, k_tm, P[2][:, 0:128])
                k.dma("gd", kn_dram[c, h], kT_n)
                tp128(v_tm, d_sb, P[2][:, 0:128])
                k.mm(P[5][:, 0:128], kT_n, kT_n)
                k.mm(P[4][:, 0:128], ones_r, grow)
                k.cp("v", col["gcb"], P[4][:, 127:128])
                k.ts("v", d_sb, P[4][:, 0:128], gcol, None, ALU.subtract)
                k.tt("v", d_sb, d_sb, lowneg, ALU.add)
                k.act(e_sb, d_sb, ACT.Exp)
                k.tt("v", uoff, e_sb, P[5][:, 0:128], ALU.mult)
                k.ts("v", uoff, uoff, bcol, None, ALU.mult)
                k.tt("v", x_sb, ident, uoff, ALU.subtract)
                k.tp(P[2][:, 0:128], uoff, ident)
                k.cp("v", ut_sb, P[2][:, 0:128])
                k.tt("v", ut_sb, ut_sb, ident, ALU.add)
                for _ in range(6):
                    tp128(xt_sb, x_sb, P[2][:, 0:128])
                    k.mm(P[6][:, 0:128], ut_sb, x_sb)
                    k.stt("v", z_sb, ident, 2.0, P[6][:, 0:128], ALU.mult, ALU.subtract)
                    k.mm(P[6][:, 0:128], xt_sb, z_sb)
                    k.cp("v", x_sb, P[6][:, 0:128])
                k.dma("gd", xi_dram[c, h], x_sb)
                k.act(col["eg"], gcol, ACT.Exp)
                k.act(col["dec"], gcol, ACT.Exp, bias=col["gcb"], scale=-1.0)
                k.act(col["egc"], col["gcb"], ACT.Exp)
                k.mm(P[5][:, 0:128], x_sb, v_tm)
                k.ts("v", rl_sb, P[5][:, 0:128], bcol, None, ALU.mult)
                k.dma("gd", rl_dram[c, h], rl_sb)
                k.ts("v", ek_sb, k_tm, col["eg"], None, ALU.mult)
                k.mm(P[6][:, 0:128], x_sb, ek_sb)
                k.ts("v", w2_sb, P[6][:, 0:128], bcol, None, ALU.mult)
                k.ts("v", kd_sb, k_tm, col["dec"], None, ALU.mult)
                k.mm(P[7][:, 0:128], kd_sb, rl_sb)
                k.cp("v", sl_sb, P[7][:, 0:128])
                k.dma("gd", sl_dram[c, h], sl_sb)
                k.mm(P[5][:, 0:128], kd_sb, w2_sb)
                k.ts("v", t2_sb, ident, col["egc"], None, ALU.mult)
                k.tt("v", mc_sb, t2_sb, P[5][:, 0:128], ALU.subtract)
                k.dma("gd", mc_dram[c, h], mc_sb)
                sM = spanM[:, h * 128:h * 128 + 128]
                sS = spanS[:, h * 128:h * 128 + 128]
                if c == 0:
                    k.cp("v", sM, mc_sb)
                    k.cp("v", sS, sl_sb)
                else:
                    tp128(xt_sb, mc_sb, P[2][:, 0:128])
                    k.mm(P[6][:, 0:128], xt_sb, sS)
                    k.tt("v", sS, P[6][:, 0:128], sl_sb, ALU.add)
                    k.mm(P[7][:, 0:128], xt_sb, sM)
                    k.cp("v", sM, P[7][:, 0:128])

        for h in range(32):
            k.dma("gd", sp_in[0, h], spanM[:, h * 128:h * 128 + 128])
            k.dma("gd", sp_in[1, h], spanS[:, h * 128:h * 128 + 128])
        k.coll("AllGather", [sp_in], [sp_out])
        S0reg = B1[:, 0:4096]
        spf = B0[:, 0:4096]
        for h in range(32):
            s0 = S0reg[:, h * 128:h * 128 + 128]
            pf = spf[:, h * 128:h * 128 + 128]
            k.memset("v", s0, 0.0)
            k.memset("v", pf, 0.0)
            for si in range(8):
                k.stt("v", s0, pf, bcm_cs[:, si:si + 1], s0, ALU.mult, ALU.add)
                if si < 7:
                    k.dma("q", kTr, sp_out[si, 0, h])
                    k.dma("q", k_tm, sp_out[si, 1, h])
                    if si == 0:
                        k.cp("v", pf, k_tm)
                    else:
                        tp128(xt_sb, kTr, P[2][:, 0:128])
                        k.mm(P[6][:, 0:128], xt_sb, pf)
                        k.tt("v", pf, P[6][:, 0:128], k_tm, ALU.add)
        if "s0" in tap_t:
            k.dma("gd", tap_t["s0"], S0reg)

        coreT = B0[:, 4096:8192]
        for c in range(8):
            for h in range(32):
                gcol = G_tm[:, c * 32 + h:c * 32 + h + 1]
                grow = rws[0:1, 0:128]
                k.dma("q", grow, gt_dram[h:h + 1, c * 128:c * 128 + 128])
                bcol = b_tm[:, c * 32 + h:c * 32 + h + 1]
                s0 = S0reg[:, h * 128:h * 128 + 128]
                k.dma("q", kTr, qkvc_dram[h // 2][:, c * 128:c * 128 + 128])
                k.dma("q", kT_n, kn_dram[c, h])
                k.dma("q", x_sb, xi_dram[c, h])
                k.dma("q", rl_sb, rl_dram[c, h])
                k.act(col["eg"], gcol, ACT.Exp)
                tp128(q_tm, kTr, P[2][:, 0:128])
                k.act(dmp, q_tm, ACT.Square, accum=col["ssq"])
                k.act(col["rstd"], col["ssq"], ACT.Sqrt, bias=ceps6)
                k.recip(col["rstd"], col["rstd"])
                k.ts("v", q_tm, q_tm, col["rstd"], RSQ, ALU.mult, ALU.mult)
                tp128(qT_n, q_tm, P[2][:, 0:128])
                k.mm(P[5][:, 0:128], kT_n, s0)
                k.ts("v", ek_sb, P[5][:, 0:128], col["eg"], None, ALU.mult)
                k.mm(P[6][:, 0:128], x_sb, ek_sb)
                k.ts("v", w2_sb, P[6][:, 0:128], bcol, None, ALU.mult)
                k.tt("v", rh_sb, rl_sb, w2_sb, ALU.subtract)
                k.mm(P[5][:, 0:128], qT_n, s0)
                k.ts("v", z_sb, P[5][:, 0:128], col["eg"], None, ALU.mult)
                k.mm(P[6][:, 0:128], qT_n, kT_n)
                k.mm(P[4][:, 0:128], ones_r, grow)
                k.ts("v", d_sb, P[4][:, 0:128], gcol, None, ALU.subtract)
                k.stt("v", d_sb, d_sb, -1.0, upneg, ALU.mult, ALU.add)
                k.act(e_sb, d_sb, ACT.Exp)
                k.tt("v", e_sb, e_sb, P[6][:, 0:128], ALU.mult)
                tp128(xt_sb, e_sb, P[2][:, 0:128])
                k.mm(P[7][:, 0:128], xt_sb, rh_sb)
                k.tt("v", z_sb, z_sb, P[7][:, 0:128], ALU.add)
                k.act(dmp, z_sb, ACT.Square, accum=col["ssq"])
                k.act(col["rstd"], col["ssq"], ACT.Sqrt, bias=ceps6, scale=float(1.0 / 128.0))
                k.recip(col["rstd"], col["rstd"])
                k.ts("v", z_sb, z_sb, col["rstd"], None, ALU.mult)
                k.tt("v", z_sb, z_sb, nwbc, ALU.mult)
                k.dma("q", e_sb, zT_dram[h][:, c * 128:c * 128 + 128])
                tp128(d_sb, e_sb, P[2][:, 0:128])
                k.tt("v", z_sb, z_sb, d_sb, ALU.mult)
                k.tp(P[2][:, 0:128], z_sb, ident)
                k.cp("v", coreT[:, h * 128:h * 128 + 128], P[2][:, 0:128])
                k.dma("q", mc_sb, mc_dram[c, h])
                k.dma("q", sl_sb, sl_dram[c, h])
                tp128(xt_sb, mc_sb, P[2][:, 0:128])
                k.mm(P[6][:, 0:128], xt_sb, s0)
                k.tt("v", s0, P[6][:, 0:128], sl_sb, ALU.add)
            for m0 in (0, 8):
                for vh in range(32):
                    wlo = wring[(2 * vh) % 4]
                    whi = wring[(2 * vh + 1) % 4]
                    k.dma("q", wlo, wout1[vh * 128:vh * 128 + 128,
                                          m0 * 128:m0 * 128 + 512])
                    k.dma("q", whi, wout1[vh * 128:vh * 128 + 128,
                                          m0 * 128 + 512:m0 * 128 + 1024])
                    for mi in range(4):
                        k.mm(P[mi][:, 0:128], wlo[:, mi * 128:mi * 128 + 128],
                             coreT[:, vh * 128:vh * 128 + 128],
                             start=(vh == 0), stop=(vh == 31))
                        k.mm(P[mi + 4][:, 0:128], whi[:, mi * 128:mi * 128 + 128],
                             coreT[:, vh * 128:vh * 128 + 128],
                             start=(vh == 0), stop=(vh == 31))
                for mi in range(8):
                    k.cp("v", C[:, (m0 + mi) * 128:(m0 + mi) * 128 + 128],
                         P[mi][:, 0:128])
            ln_chunk(c)
        if "x3" in tap_t:
            k.dma("gd", tap_t["x3"], x3_dram)

    def ln_chunk(c):
        xld = C[:, 2048:4096]
        k.dma("q", xld, x2_dram[c * 128:c * 128 + 128, :])
        for m in range(16):
            sl = (m % 4) * 128
            k.tp(P[2][:, sl:sl + 128], C[:, m * 128:m * 128 + 128], ident)
            k.cp("v", wsta[:, sl:sl + 128], P[2][:, sl:sl + 128])
            if m % 4 == 3:
                mm0 = (m // 4) * 512
                k.tt("v", xld[:, mm0:mm0 + 512], xld[:, mm0:mm0 + 512], wsta, ALU.add)
        k.red("v", col["t0"], xld, ALU.add, neg=True)
        k.ts("v", col["t1"], col["t0"], float(1.0 / 2048.0), None, ALU.mult)
        k.ts("v", xld, xld, col["t1"], None, ALU.add)
        k.act(C[:, 0:2048], xld, ACT.Square, accum=col["ssq"])
        k.act(col["rstd"], col["ssq"], ACT.Sqrt, bias=ceps5, scale=float(1.0 / 2048.0))
        k.recip(col["rstd"], col["rstd"])
        k.ts("v", xld, xld, col["rstd"], None, ALU.mult)
        k.dma("gd", x3_dram[c * 128:c * 128 + 128, :], xld)

    def l1_ffn():
        def gamma_of(wap, kk_tiles, width, gw_col):
            for i in range(kk_tiles):
                k.dma("q", A0[:, 0:width], wap[i * 128:i * 128 + 128, :])
                k.red("v", col["t0"], A0[:, 0:width], ALU.add, absolute=True)
                if i == 0:
                    k.cp("v", col["t1"], col["t0"])
                else:
                    k.tt("v", col["t1"], col["t1"], col["t0"], ALU.add)
            k.mm(P[4][0:1, 0:1], ones_c, col["t1"])
            k.ts("v", rws[0:1, 0:1], P[4][0:1, 0:1], float(1.0 / (D * 8192)), 1e-5,
                 ALU.mult, ALU.max)
            bcast_col(rws[0:1, 0:1], gw_col, P[3])

        gamma_of(wgT, 16, 8192, gw_g)
        gamma_of(wvT, 16, 8192, gw_v)
        gamma_of(woT, 64, 2048, gw_o)

        def quant_w(wap, dst, kk_tiles, width, gw_col):
            k.recip(col["t0"], gw_col)
            for i in range(kk_tiles):
                for j0 in range(0, width, 512):
                    k.dma("q", wsta, wap[i * 128:i * 128 + 128, j0:j0 + 512])
                    k.ts("v", qsc, wsta, col["t0"], None, ALU.mult)
                    k.cp("v", isc, qsc)
                    k.cp("v", qsc, isc)
                    k.ts("v", qsc, qsc, 1.0, -1.0, ALU.min, ALU.max)
                    k.cp("v", wqbf, qsc)
                    k.dma("gd", dst[i * 128:i * 128 + 128, j0:j0 + 512], wqbf)

        quant_w(wgT, wqg_dram, 16, 8192, gw_g)
        quant_w(wvT, wqv_dram, 16, 8192, gw_v)
        quant_w(woT, wqo_dram, 64, 2048, gw_o)

        for th in range(2):
            for jj in range(4):
                j = th * 4 + jj
                xld = B0[:, 0:2048]
                k.dma("q", xld, x3_dram[j * 128:j * 128 + 128, :])
                k.red("v", col["gx"], xld, ALU.max, absolute=True)
                k.ts("v", col["gx"], col["gx"], 1e-5, None, ALU.max)
                k.recip(col["sx"], col["gx"])
                k.ts("v", col["sx"], col["sx"], 127.0, None, ALU.mult)
                k.ts("v", col["t0"], col["gx"], float(1.0 / 127.0), None, ALU.mult)
                k.tp(P[4][0:1, jj * 128:jj * 128 + 128], col["t0"], ident)
                for p in range(4):
                    k.ts("v", qsc, xld[:, p * 512:p * 512 + 512], col["sx"], None,
                         ALU.mult)
                    k.cp("v", isc, qsc)
                    k.cp("v", qsc, isc)
                    for b2 in range(4):
                        kk = p * 4 + b2
                        k.tp(P[2][:, 0:128], qsc[:, b2 * 128:b2 * 128 + 128], ident)
                        k.cp("v", BFq[:, kk * 512 + jj * 128:kk * 512 + jj * 128 + 128],
                             P[2][:, 0:128])
            k.cp("v", rws[0:1, 0:512], P[4][0:1, 0:512])
            bcg = B0[:, 2048:2560]
            bcv = B0[:, 2560:3072]
            k.mm(P[3][0:128, 0:512], ones_r, rws[0:1, 0:512])
            k.ts("v", bcg, P[3][0:128, 0:512], gw_g, None, ALU.mult)
            k.ts("v", bcv, P[3][0:128, 0:512], gw_v, None, ALU.mult)
            zmx = B0[:, 3072:3584]
            zmn = B0[:, 3584:4096]
            zt = B0[:, 4096:4608]
            for ht in range(64):
                for kk in range(16):
                    k.dma("q", wqbf[:, 0:128],
                          wqg_dram[kk * 128:kk * 128 + 128, ht * 128:ht * 128 + 128])
                    k.mm(P[0], wqbf[:, 0:128], BFq[:, kk * 512:kk * 512 + 512],
                         start=(kk == 0), stop=(kk == 15))
                    k.dma("q", zqbf[:, 0:128],
                          wqv_dram[kk * 128:kk * 128 + 128, ht * 128:ht * 128 + 128])
                    k.mm(P[1], zqbf[:, 0:128], BFq[:, kk * 512:kk * 512 + 512],
                         start=(kk == 0), stop=(kk == 15))
                k.tt("v", wsta, P[0], bcg, ALU.mult)
                k.ts("v", wsta, wsta, bg_sb[:, ht:ht + 1], None, ALU.add)
                k.act(wsta, wsta, ACT.Silu)
                k.tt("v", wstb, P[1], bcv, ALU.mult)
                k.ts("v", wstb, wstb, bv_sb[:, ht:ht + 1], None, ALU.add)
                k.tt("v", zt, wsta, wstb, ALU.mult)
                k.dma("gd", z_dram[ht], zt)
                if ht == 0:
                    k.cp("v", zmx, zt)
                    k.cp("v", zmn, zt)
                else:
                    k.tt("v", zmx, zmx, zt, ALU.max)
                    k.tt("v", zmn, zmn, zt, ALU.min)
            for jj in range(4):
                k.tp(P[2][:, 0:128], zmx[:, jj * 128:jj * 128 + 128], ident)
                k.cp("v", pts, P[2][:, 0:128])
                k.red("v", col["t0"], pts, ALU.max)
                k.tp(P[2][:, 0:128], zmn[:, jj * 128:jj * 128 + 128], ident)
                k.cp("v", pts, P[2][:, 0:128])
                k.red("v", col["t1"], pts, ALU.max, absolute=True)
                k.tt("v", col["t0"], col["t0"], col["t1"], ALU.max)
                k.ts("v", col["gz"], col["t0"], 1e-5, None, ALU.max)
                k.recip(col["sxz"], col["gz"])
                k.ts("v", col["sxz"], col["sxz"], 127.0, None, ALU.mult)
                k.ts("v", col["t1"], col["gz"], float(1.0 / 127.0), None, ALU.mult)
                k.tp(P[4][0:1, jj * 128:jj * 128 + 128], col["sxz"], ident)
                k.tp(P[5][0:1, jj * 128:jj * 128 + 128], col["t1"], ident)
            k.cp("v", rws[0:1, 0:512], P[4][0:1, 0:512])
            sxzbc = B0[:, 4608:5120]
            k.mm(P[3][0:128, 0:512], ones_r, rws[0:1, 0:512])
            k.cp("v", sxzbc, P[3][0:128, 0:512])
            k.cp("v", rws[0:1, 512:1024], P[5][0:1, 0:512])
            bco = B0[:, 5120:5632]
            k.mm(P[3][0:128, 0:512], ones_r, rws[0:1, 512:1024])
            k.ts("v", bco, P[3][0:128, 0:512], gw_o, None, ALU.mult)
            for mg in range(2):
                for kk in range(64):
                    k.dma("q", zt, z_dram[kk])
                    k.tt("v", qsc, zt, sxzbc, ALU.mult)
                    k.cp("v", isc, qsc)
                    k.cp("v", qsc, isc)
                    k.cp("v", zqbf, qsc)
                    for mi in range(8):
                        m = mg * 8 + mi
                        k.dma("q", wqbf[:, 0:128],
                              wqo_dram[kk * 128:kk * 128 + 128, m * 128:m * 128 + 128])
                        k.mm(P[mi], wqbf[:, 0:128], zqbf,
                             start=(kk == 0), stop=(kk == 63))
                for mi in range(8):
                    m = mg * 8 + mi
                    ct = C[:, m * 512:m * 512 + 512]
                    k.tt("v", ct, P[mi], bco, ALU.mult)
                    k.ts("v", ct, ct, bo_sb[:, m:m + 1], None, ALU.add)
            ln_step(th, x3_dram, out_o, build_xt=False)

    l0_attn()
    if STAGES not in ("l0a",):
        l0_ffn()
    if STAGES not in ("l0a", "l0"):
        l1_attn()
    if STAGES == "all":
        l1_ffn()

    for ln in range(4):
        if k.seq[f"gd{ln}"]:
            k.eng["g"].wait_ge(k.sem[f"gd{ln}"], k._cover(f"gd{ln}", k.seq[f"gd{ln}"]))
        if k.seq[f"q{ln}"]:
            k.eng["g"].wait_ge(k.sem[f"q{ln}"], k._cover(f"q{ln}", k.seq[f"q{ln}"]))

    k.ctx.close()
    return nc


def _prep_inputs(inputs):
    f = lambda a: np.ascontiguousarray(np.asarray(a, dtype=np.float32))
    x = np.asarray(inputs["L0_attn_x"]).reshape(S, D)
    wq = np.asarray(inputs["L1_attn_W_qkvz"])
    perm = np.zeros(12288, dtype=np.int64)
    for kh in range(16):
        base = 768 * kh
        perm[128 * kh:128 * kh + 128] = np.arange(base, base + 128)
        perm[2048 + 128 * kh:2048 + 128 * kh + 128] = np.arange(base + 128, base + 256)
        perm[4096 + 256 * kh:4096 + 256 * kh + 256] = np.arange(base + 256, base + 512)
        perm[8192 + 256 * kh:8192 + 256 * kh + 256] = np.arange(base + 512, base + 768)
    wqkvz = f(wq[:, perm])
    wba = np.asarray(inputs["L1_attn_W_ba"])
    bidx = np.array([4 * kh + r for kh in range(16) for r in (0, 1)])
    aidx = np.array([4 * kh + 2 + r for kh in range(16) for r in (0, 1)])
    cw = np.asarray(inputs["L1_attn_conv_weight"]).reshape(64, 128, 4)
    shared = {
        "wdq": f(inputs["L0_attn_W_dq"]), "wuq": f(inputs["L0_attn_W_uq"]),
        "wdkv": f(inputs["L0_attn_W_dkv"]), "wuk": f(inputs["L0_attn_W_uk"]),
        "wuv": f(inputs["L0_attn_W_uv"]), "wqr": f(inputs["L0_attn_W_qr"]),
        "wkr": f(inputs["L0_attn_W_kr"]), "wout0": f(inputs["L0_attn_W_out"]),
        "gq": f(np.asarray(inputs["L0_attn_g_q"]).reshape(4, 128).T),
        "gkv": f(np.asarray(inputs["L0_attn_g_kv"]).reshape(4, 128).T),
        "bout0": f(np.asarray(inputs["L0_attn_b_out"]).reshape(16, 128).T),
        "gwT": f(np.asarray(inputs["L0_ffn_gate_w"]).T),
        "efc1T": f(np.transpose(np.asarray(inputs["L0_ffn_expert_fc1"]), (0, 2, 1))),
        "efc2T": f(np.transpose(np.asarray(inputs["L0_ffn_expert_fc2"]), (0, 2, 1))),
        "eprojT": f(np.transpose(np.asarray(inputs["L0_ffn_expert_proj"]), (0, 2, 1))),
        "sfc1T": f(np.asarray(inputs["L0_ffn_shared_fc1"]).T),
        "sfc2T": f(np.asarray(inputs["L0_ffn_shared_fc2"]).T),
        "sprojT": f(np.asarray(inputs["L0_ffn_shared_proj"]).T),
        "sgwT": f(np.asarray(inputs["L0_ffn_shared_gate_w"]).T),
        "wqkvz": wqkvz,
        "wbab": f(wba[:, bidx]), "wbaa": f(wba[:, aidx]),
        "cwt": f(np.transpose(cw, (1, 0, 2)).reshape(128, 256)),
        "alog": f(np.asarray(inputs["L1_attn_A_log"]).reshape(32, 1)),
        "dtb": f(np.asarray(inputs["L1_attn_dt_bias"]).reshape(32, 1)),
        "nw": f(np.asarray(inputs["L1_attn_norm_weight"]).reshape(1, 128)),
        "wout1": f(inputs["L1_attn_W_out"]),
        "wgT": f(np.asarray(inputs["L1_ffn_gate_w"]).T),
        "wvT": f(np.asarray(inputs["L1_ffn_val_w"]).T),
        "woT": f(np.asarray(inputs["L1_ffn_out_w"]).T),
        "bg": f(np.asarray(inputs["L1_ffn_gate_b"]).reshape(64, 128).T),
        "bv": f(np.asarray(inputs["L1_ffn_val_b"]).reshape(64, 128).T),
        "bo": f(np.asarray(inputs["L1_ffn_out_b"]).reshape(16, 128).T),
    }
    in_maps = []
    for c in range(NCORES):
        m = dict(shared)
        m["x"] = f(x[c * TOK:(c + 1) * TOK, :])
        m["thr"] = np.arange(c * TOK, (c + 1) * TOK,
                             dtype=np.float32).reshape(TOK, 1).copy()
        oh = np.zeros((1, 8), np.float32); oh[0, c] = 1.0
        m["mcs"] = oh
        bd = np.zeros((1, 8), np.float32)
        if c > 0:
            bd[0, c - 1] = 1.0
        m["mbd"] = bd
        in_maps.append(m)
    return in_maps


def _run(inputs, taps=()):
    nc = build_program(tuple(taps))
    in_maps = _prep_inputs(inputs)
    res = run_bass_kernel_spmd(nc, in_maps, core_ids=list(range(NCORES)))
    outs = [np.asarray(r["o"]) for r in res.results]
    full = np.concatenate(outs, axis=0).reshape(1, S, D).astype(np.float32)
    tapvals = [{name: np.asarray(r["tap_" + name]) for name in taps}
               for r in res.results]
    return full, tapvals


def kernel(**inputs):
    out, _ = _run(inputs, taps=TAPS)
    return out


if __name__ == "__main__":
    nc = build_program(())
    print("built", sum(1 for _ in nc.all_instructions()), "instructions")
